# version 15
# baseline (speedup 1.0000x reference)
import numpy as np
import ml_dtypes
from contextlib import ExitStack

import concourse.bass as bass
import concourse.mybir as mybir
from concourse.bass_utils import run_bass_kernel_spmd

F32 = mybir.dt.float32
BF16 = mybir.dt.bfloat16
I32 = mybir.dt.int32
AF = mybir.ActivationFunctionType
ALU = mybir.AluOpType
AX = mybir.AxisListType
BF = ml_dtypes.bfloat16

B = 8
T = 2048
D = 4096
HD = 128
NH = 32
NKV = 8
KC = D // 128
NT = T // 128
SW = 512
NSLAB = T // SW
ISQ = float(1.0 / np.sqrt(128.0))
EPS = 1e-5
ROPE_BASE = 500000.0


def _host_tables():
    inv_freq = (1.0 / (ROPE_BASE ** (np.arange(0, HD, 2, dtype=np.float32) / HD))).astype(np.float32)
    t = np.arange(T, dtype=np.float32)
    freqs = np.einsum("i,j->ij", t, inv_freq).astype(np.float32)
    cos = np.cos(freqs).astype(np.float32)
    sin = np.sin(freqs).astype(np.float32)
    cosT = cos.T.copy()
    sinT = sin.T.copy()
    cos2 = np.concatenate([cosT, cosT], axis=0)
    masks = np.zeros((4, 128, SW), dtype=BF)
    for r in range(4):
        kl = np.arange(128)[:, None]
        ql = np.arange(SW)[None, :]
        masks[r] = ((r * 128 + kl) <= ql).astype(BF)
    return cos2, sinT, masks


def build_nc(upto="p7", debug=False):
    nc = bass.Bass("TRN2", target_bir_lowering=False)

    x_in = nc.dram_tensor("x", [T, D], F32, kind="ExternalInput")
    Wq = nc.dram_tensor("Wq", [D, D], F32, kind="ExternalInput")
    Wk = nc.dram_tensor("Wk", [D, NKV * HD], F32, kind="ExternalInput")
    Wv = nc.dram_tensor("Wv", [D, NKV * HD], F32, kind="ExternalInput")
    Wo = nc.dram_tensor("Wo", [D, D], F32, kind="ExternalInput")
    Wf = nc.dram_tensor("Wf", [D, D], F32, kind="ExternalInput")
    bias_t = nc.dram_tensor("fbias", [1, D], F32, kind="ExternalInput")
    lA = nc.dram_tensor("lA", [3, 16, D], F32, kind="ExternalInput")
    lB = nc.dram_tensor("lB", [3, D, 16], F32, kind="ExternalInput")
    lsi_t = nc.dram_tensor("lsi", [1, 4], I32, kind="ExternalInput")
    bsel_t = nc.dram_tensor("bsel", [1, 8], F32, kind="ExternalInput")
    out_t = nc.dram_tensor("out", [T, D], F32, kind="ExternalOutput")
    dbg = {}
    if debug:
        dbg["qT"] = nc.dram_tensor("d_qT", [NH, 128, T], F32, kind="ExternalOutput")
        dbg["kT"] = nc.dram_tensor("d_kT", [NKV, 128, T], F32, kind="ExternalOutput")
        dbg["vT"] = nc.dram_tensor("d_vT", [NKV, 128, T], F32, kind="ExternalOutput")
        dbg["x2"] = nc.dram_tensor("d_x2", [T, D], F32, kind="ExternalOutput")

    qT_s = nc.dram_tensor("qT_s", [NH, 128, T], BF16, kind="Internal")
    kT_s = nc.dram_tensor("kT_s", [NKV, 128, T], BF16, kind="Internal")
    vT_s = nc.dram_tensor("vT_s", [NKV, 128, T], BF16, kind="Internal")
    x2_s = nc.dram_tensor("x2_s", [T, D], F32, kind="Internal")
    WfB = nc.dram_tensor("WfB", [D, D], BF16, kind="Internal")

    cos2_np, sinT_np, masks_np = _host_tables()
    ident_c = nc.inline_tensor(np.eye(128).astype(BF), name="identc")
    iota8_c = nc.inline_tensor(np.arange(8, dtype=np.float32).reshape(1, 8), name="iota8c")
    onesb_c = nc.inline_tensor(np.ones((128, 1), dtype=BF), name="onesbc")
    onesq_c = nc.inline_tensor(np.ones((128, 128), dtype=BF), name="onesqc")
    onesf_c = nc.inline_tensor(np.ones((1, 128), dtype=np.float32), name="onesfc")
    cos2_c = nc.inline_tensor(cos2_np, name="cos2c")
    sinT_c = nc.inline_tensor(sinT_np, name="sinTc")
    mask_c = nc.inline_tensor(masks_np.reshape(4 * 128, SW), name="maskc")

    actT = nc.alloc_sbuf_tensor("actT", [128, KC * T], BF16)
    idt = nc.alloc_sbuf_tensor("idt", [128, 128], BF16)
    onesb = nc.alloc_sbuf_tensor("onesb", [128, 1], BF16)
    onesq = nc.alloc_sbuf_tensor("onesq", [128, 128], BF16)
    onesf = nc.alloc_sbuf_tensor("onesf", [1, 128], F32)
    iota8 = nc.alloc_sbuf_tensor("iota8", [1, 8], F32)
    bsel_sb = nc.alloc_sbuf_tensor("bsel_sb", [1, 8], F32)
    lsi_i = nc.alloc_sbuf_tensor("lsi_i", [1, 4], I32)
    lsi_f = nc.alloc_sbuf_tensor("lsi_f", [1, 4], F32)
    mrow = nc.alloc_sbuf_tensor("mrow", [1, 3], F32)
    rstd2 = nc.alloc_sbuf_tensor("rstd2", [128, NT], F32)
    negmr2 = nc.alloc_sbuf_tensor("negmr2", [128, NT], F32)
    s1acc = nc.alloc_sbuf_tensor("s1acc", [128, NT], F32)
    s2acc = nc.alloc_sbuf_tensor("s2acc", [128, NT], F32)

    SEM = {}

    def sem(name):
        if name not in SEM:
            SEM[name] = nc.alloc_semaphore(name)
        return SEM[name]

    def ser(eng, name, j, inst_fn):
        if j > 0:
            eng.wait_ge(sem(name), j)
        inst_fn().then_inc(sem(name), 1)
        return j + 1

    def run_oplist(block, name, oplist):
        for engname in ["sync", "gpsimd", "tensor", "vector", "scalar"]:
            ops = [(j, fn) for j, (en, fn) in enumerate(oplist) if en == engname]
            if not ops:
                continue

            def emit(e, ops=ops):
                for j, fn in ops:
                    if j > 0:
                        e.wait_ge(sem(name), j)
                    fn(e).then_inc(sem(name), 1)

            getattr(block, engname)(emit)

    es_a = ExitStack()
    cos2 = es_a.enter_context(nc.sbuf_tensor("cos2", [128, T], F32))
    sinT = es_a.enter_context(nc.sbuf_tensor("sinT", [64, T], F32))
    tmp8 = es_a.enter_context(nc.sbuf_tensor("tmp8", [1, 8 * 12], F32))
    rtt = es_a.enter_context(nc.sbuf_tensor("rtt", [128, 512], F32))
    rkc = es_a.enter_context(nc.sbuf_tensor("rkc", [128, 512], F32))

    def t8(i):
        return tmp8[0:1, i * 8:(i + 1) * 8]

    with nc.Block("p0") as block:

        @block.sync
        def _(sync):
            sync.dma_start(out=idt[:], in_=ident_c.ap()).then_inc(sem("l0"), 16)
            sync.dma_start(out=onesb[:], in_=onesb_c.ap()).then_inc(sem("l0"), 16)
            sync.dma_start(out=onesq[:], in_=onesq_c.ap()).then_inc(sem("l0"), 16)
            sync.dma_start(out=onesf[:], in_=onesf_c.ap()).then_inc(sem("l0"), 16)
            sync.dma_start(out=iota8[:], in_=iota8_c.ap()).then_inc(sem("l0"), 16)
            sync.dma_start(out=bsel_sb[:], in_=bsel_t.ap()).then_inc(sem("l0"), 16)
            sync.dma_start(out=lsi_i[:], in_=lsi_t.ap()).then_inc(sem("l0"), 16)
            sync.dma_start(out=cos2[:], in_=cos2_c.ap()).then_inc(sem("l0"), 16)
            sync.dma_start(out=sinT[:], in_=sinT_c.ap()).then_inc(sem("l0"), 16)

        @block.vector
        def _(vector):
            j = 0
            vector.wait_ge(sem("l0"), 16 * 7)
            j = ser(vector, "z0", j, lambda: vector.tensor_copy(out=lsi_f[:], in_=lsi_i[:]))
            for i in range(4):
                j = ser(vector, "z0", j, lambda i=i: vector.tensor_single_scalar(
                    out=t8(i), in_=iota8[:], scalar=lsi_f[0:1, i:i + 1], op=ALU.is_lt))
            j = ser(vector, "z0", j, lambda: vector.scalar_tensor_tensor(
                out=t8(4), in0=t8(0), scalar=-1.0, in1=t8(1), op0=ALU.mult, op1=ALU.mult))
            j = ser(vector, "z0", j, lambda: vector.tensor_tensor(out=t8(4), in0=t8(4), in1=t8(1), op=ALU.add))
            j = ser(vector, "z0", j, lambda: vector.tensor_scalar(
                out=t8(5), in0=t8(1), scalar1=-2.0, scalar2=2.0, op0=ALU.mult, op1=ALU.add))
            j = ser(vector, "z0", j, lambda: vector.tensor_tensor(out=t8(4), in0=t8(4), in1=t8(5), op=ALU.add))
            j = ser(vector, "z0", j, lambda: vector.tensor_scalar(
                out=t8(6), in0=t8(3), scalar1=-1.0, scalar2=4.0, op0=ALU.mult, op1=ALU.add))
            j = ser(vector, "z0", j, lambda: vector.tensor_tensor(out=t8(7), in0=t8(4), in1=t8(2), op=ALU.mult))
            j = ser(vector, "z0", j, lambda: vector.tensor_scalar(
                out=t8(8), in0=t8(2), scalar1=-1.0, scalar2=1.0, op0=ALU.mult, op1=ALU.add))
            j = ser(vector, "z0", j, lambda: vector.tensor_tensor(out=t8(8), in0=t8(8), in1=t8(6), op=ALU.mult))
            j = ser(vector, "z0", j, lambda: vector.tensor_tensor(out=t8(7), in0=t8(7), in1=t8(8), op=ALU.add))
            j = ser(vector, "z0", j, lambda: vector.tensor_single_scalar(
                out=t8(9), in_=t8(7), scalar=1.5, op=ALU.is_ge))
            j = ser(vector, "z0", j, lambda: vector.tensor_tensor(out=t8(9), in0=t8(9), in1=bsel_sb[:], op=ALU.mult))
            j = ser(vector, "z0", j, lambda: vector.tensor_scalar(
                out=t8(10), in0=t8(7), scalar1=-2.0, scalar2=0.0, op0=ALU.add, op1=ALU.max))
            j = ser(vector, "z0", j, lambda: vector.tensor_single_scalar(
                out=t8(10), in_=t8(10), scalar=2.0, op=ALU.min))
            for a in range(3):
                j = ser(vector, "z0", j, lambda a=a: vector.tensor_single_scalar(
                    out=t8(11), in_=t8(10), scalar=float(a), op=ALU.is_equal))
                j = ser(vector, "z0", j, lambda: vector.tensor_tensor(
                    out=t8(11), in0=t8(11), in1=t8(9), op=ALU.mult))
                j = ser(vector, "z0", j, lambda a=a: vector.reduce_sum(
                    out=mrow[0:1, a:a + 1], in_=t8(11), axis=AX.X))

    if upto == "p0":
        _emit_debug_tail(nc, sem, actT, out_t, dbg, qT_s, kT_s, vT_s, x2_s)
        return nc

    C1 = 10
    es_p1 = ExitStack()
    xt = [es_p1.enter_context(nc.sbuf_tensor(f"xt{i}", [128, D], F32)) for i in range(2)]
    xsq = [es_p1.enter_context(nc.sbuf_tensor("xsq0", [128, D], F32))] * 2
    h1b = [es_p1.enter_context(nc.sbuf_tensor("h1b0", [128, D], BF16))] * 2
    stt = es_p1.enter_context(nc.sbuf_tensor("sttiles", [128, 2, 8], F32))

    def stc(i, k):
        return stt[:, i, k:k + 1]

    with nc.Block("p1") as block:
        pt1 = [es_p1.enter_context(nc.psum_tensor(f"pt1_{i}", [128, 512], BF16)) for i in range(2)]

        @block.sync
        def _(sync):
            for t in range(NT):
                if t >= 2:
                    sync.wait_ge(sem("ap1"), t - 1)
                sync.dma_start(out=xt[t % 2][:], in_=x_in.ap()[t * 128:(t + 1) * 128, :]).then_inc(sem("xld1"), 16)

        @block.vector
        def _(vector):
            for t in range(NT):
                i = t % 2
                bt = 6 * t
                vector.wait_ge(sem("xld1"), 16 * (t + 1))
                if t >= 2:
                    vector.wait_ge(sem("ap1"), t - 1)
                vector.reduce_sum(out=stc(i, 0), in_=xt[i][:], axis=AX.X).then_inc(sem("st1"), 1)
                vector.wait_ge(sem("st1"), t + 1)
                if bt > 0:
                    vector.wait_ge(sem("sC1"), bt)
                vector.tensor_scalar_mul(out=stc(i, 2), in0=stc(i, 0), scalar1=1.0 / D).then_inc(sem("sC1"), 1)
                vector.wait_ge(sem("sA1"), t + 1)
                vector.wait_ge(sem("sC1"), bt + 1)
                vector.tensor_scalar_mul(out=stc(i, 3), in0=stc(i, 1), scalar1=1.0 / D).then_inc(sem("sC1"), 1)
                vector.wait_ge(sem("sC1"), bt + 2)
                vector.scalar_tensor_tensor(
                    out=stc(i, 4), in0=stc(i, 2), scalar=stc(i, 2), in1=stc(i, 3),
                    op0=ALU.mult, op1=ALU.subtract).then_inc(sem("sC1"), 1)
                vector.wait_ge(sem("sC1"), bt + 3)
                vector.tensor_scalar(
                    out=stc(i, 4), in0=stc(i, 4), scalar1=-1.0, scalar2=EPS,
                    op0=ALU.mult, op1=ALU.add).then_inc(sem("sC1"), 1)
                vector.wait_ge(sem("sD1"), t + 1)
                vector.reciprocal(out=stc(i, 6), in_=stc(i, 5)).then_inc(sem("sC1"), 1)
                vector.wait_ge(sem("sC1"), bt + 5)
                vector.scalar_tensor_tensor(
                    out=stc(i, 7), in0=stc(i, 2), scalar=-1.0, in1=stc(i, 6),
                    op0=ALU.mult, op1=ALU.mult).then_inc(sem("sC1"), 1)

        @block.scalar
        def _(scalar):
            for t in range(NT):
                i = t % 2
                bt = 6 * t
                scalar.wait_ge(sem("xld1"), 16 * (t + 1))
                if t >= 2:
                    scalar.wait_ge(sem("sC1"), 6 * (t - 2) + 2)
                scalar.activation(
                    out=xsq[i][:], in_=xt[i][:], func=AF.Square,
                    accum_out=stc(i, 1)).then_inc(sem("sA1"), 1)
                scalar.wait_ge(sem("sC1"), bt + 4)
                scalar.activation(
                    out=stc(i, 5), in_=stc(i, 4), func=AF.Sqrt).then_inc(sem("sD1"), 1)
                scalar.wait_ge(sem("sC1"), bt + 6)
                if t >= 1:
                    scalar.wait_ge(sem("tr1"), 8 * t)
                scalar.activation(
                    out=h1b[i][:], in_=xt[i][:], func=AF.Identity,
                    scale=stc(i, 6), bias=stc(i, 7),
                ).then_inc(sem("ap1"), 1)
                for g in range(8):
                    gi = t * 8 + g
                    scalar.wait_ge(sem("tr1"), gi + 1)
                    dst = actT.ap().rearrange("p (c q) -> p c q", q=T)[:, 4 * g:4 * g + 4, t * 128:(t + 1) * 128]
                    scalar.copy(out=dst, in_=pt1[g % 2].ap().rearrange("p (c q) -> p c q", q=128)).then_inc(sem("ev1"), 1)

        @block.tensor
        def _(tensor):
            for t in range(NT):
                i = t % 2
                tensor.wait_ge(sem("ap1"), t + 1)
                for g in range(8):
                    gi = t * 8 + g
                    if gi >= 2:
                        tensor.wait_ge(sem("ev1"), gi - 1)
                    for j in range(4):
                        c = 4 * g + j
                        inst = tensor.transpose(
                            pt1[g % 2][:, j * 128:(j + 1) * 128],
                            h1b[i][:, c * 128:(c + 1) * 128],
                            idt[:],
                        )
                        if j == 3:
                            inst.then_inc(sem("tr1"), 1)

        es_p1.close()

    if upto == "p1":
        _emit_debug_tail(nc, sem, actT, out_t, dbg, qT_s, kT_s, vT_s, x2_s)
        return nc

    UNITS2 = []
    for m in range(NKV):
        UNITS2.append(("k", m))
        UNITS2.append(("v", m))
    with nc.Block("p2") as block:
        es_p2 = ExitStack()
        wub = [es_p2.enter_context(nc.sbuf_tensor(f"wub{i}", [128, KC, 128], BF16)) for i in range(2)]
        kro = es_p2.enter_context(nc.sbuf_tensor("kro0", [128, T], BF16))
        vbs = es_p2.enter_context(nc.sbuf_tensor("vbs0", [128, T], BF16))
        wfc = es_p2.enter_context(nc.sbuf_tensor("wfc0", [128, D], BF16))
        ps_u = [es_p2.enter_context(nc.psum_tensor(f"ps_u{i}", [128, T], F32)) for i in range(2)]

        @block.gpsimd
        def _(gpsimd):
            for u, (typ, m) in enumerate(UNITS2):
                if u >= 2:
                    gpsimd.wait_ge(sem("mm2"), u - 1)
                W_src = {"k": Wk, "v": Wv}[typ]
                for half in range(2):
                    srcw = W_src.ap().rearrange("(k p) m -> p k m", p=128)[:, half * 16:(half + 1) * 16, m * 128:(m + 1) * 128]
                    gpsimd.dma_start(out=wub[u % 2][:, half * 16:(half + 1) * 16, :], in_=srcw).then_inc(sem("wk2"), 16)
                if typ == "v":
                    for j in range(2):
                        cc = m * 2 + j
                        for sub in range(2):
                            c = cc * 2 + sub
                            if c >= 1:
                                gpsimd.wait_ge(sem("wfw2"), 16 * c)
                            gpsimd.dma_start(out=wfc[:], in_=Wf.ap()[c * 128:(c + 1) * 128, :]).then_inc(sem("wfc2"), 16)

        @block.scalar
        def _(scalar):
            nv = 0
            for u, (typ, m) in enumerate(UNITS2):
                if typ == "v":
                    for j in range(4):
                        c = m * 4 + j
                        scalar.wait_ge(sem("wfc2"), 16 * (c + 1))
                        scalar.dma_start(out=WfB.ap()[c * 128:(c + 1) * 128, :], in_=wfc[:]).then_inc(sem("wfw2"), 16)
                    nv += 1
                    scalar.wait_ge(sem("mm2"), u + 1)
                    if nv >= 2:
                        scalar.wait_ge(sem("vw2"), 16 * (nv - 1))
                    scalar.activation(out=vbs[:], in_=ps_u[u % 2][:], func=AF.Copy).then_inc(sem("vc2"), 1)

        @block.tensor
        def _(tensor):
            for u, (typ, m) in enumerate(UNITS2):
                tensor.wait_ge(sem("wk2"), 32 * (u + 1))
                if u >= 2:
                    prev = u - 2
                    if prev % 2 == 0:
                        tensor.wait_ge(sem("rp2"), 20 * (prev // 2) + 18)
                    else:
                        tensor.wait_ge(sem("vc2"), (prev + 1) // 2)
                for k in range(KC):
                    for s4 in range(4):
                        inst = tensor.matmul(
                            ps_u[u % 2][:, s4 * 512:(s4 + 1) * 512], wub[u % 2][:, k, :],
                            actT[:, k * T + s4 * 512: k * T + (s4 + 1) * 512],
                            start=(k == 0), stop=(k == KC - 1),
                        )
                        if k == KC - 1 and s4 == 3:
                            inst.then_inc(sem("mm2"), 1)

        @block.vector
        def _(vector):
            nk = 0
            for u, (typ, m) in enumerate(UNITS2):
                if typ == "v":
                    continue
                nk += 1
                vector.wait_ge(sem("mm2"), u + 1)
                if nk >= 2:
                    vector.wait_ge(sem("kw2"), 16 * (nk - 1))
                base = 20 * (nk - 1)
                for sl in range(4):
                    L = sl * 512
                    R = (sl + 1) * 512
                    bb = base + 5 * sl
                    ser(vector, "rp2", bb + 0, lambda u=u, L=L, R=R: vector.tensor_tensor(
                        out=rtt[0:64, :], in0=ps_u[u % 2][64:128, L:R], in1=sinT[:, L:R], op=ALU.mult))
                    ser(vector, "rp2", bb + 1, lambda u=u, L=L, R=R: vector.tensor_tensor(
                        out=rtt[64:128, :], in0=ps_u[u % 2][0:64, L:R], in1=sinT[:, L:R], op=ALU.mult))
                    ser(vector, "rp2", bb + 2, lambda u=u, L=L, R=R: vector.tensor_tensor(
                        out=rkc[:], in0=ps_u[u % 2][:, L:R], in1=cos2[:, L:R], op=ALU.mult))
                    ser(vector, "rp2", bb + 3, lambda L=L, R=R: vector.tensor_tensor(
                        out=kro[0:64, L:R], in0=rkc[0:64, :], in1=rtt[0:64, :], op=ALU.subtract))
                    ser(vector, "rp2", bb + 4, lambda L=L, R=R: vector.tensor_tensor(
                        out=kro[64:128, L:R], in0=rkc[64:128, :], in1=rtt[64:128, :], op=ALU.add))

        @block.sync
        def _(sync):
            nk = 0
            nv = 0
            for u, (typ, m) in enumerate(UNITS2):
                if typ == "k":
                    nk += 1
                    sync.wait_ge(sem("rp2"), 20 * nk)
                    sync.dma_start(out=kT_s.ap()[m], in_=kro[:]).then_inc(sem("kw2"), 16)
                else:
                    nv += 1
                    sync.wait_ge(sem("vc2"), nv)
                    sync.dma_start(out=vT_s.ap()[m], in_=vbs[:]).then_inc(sem("vw2"), 16)

        es_p2.close()

    if upto == "p2":
        _emit_debug_tail(nc, sem, actT, out_t, dbg, qT_s, kT_s, vT_s, x2_s)
        return nc

    with nc.Block("p3") as block:
        es_p3 = ExitStack()
        wqb = [es_p3.enter_context(nc.sbuf_tensor(f"wqb{i}", [128, KC, 128], BF16)) for i in range(2)]
        qro = [es_p3.enter_context(nc.sbuf_tensor(f"qro{i}", [128, T], BF16)) for i in range(2)]
        ps_q = [es_p3.enter_context(nc.psum_tensor(f"ps_q{i}", [128, T], F32)) for i in range(2)]

        @block.gpsimd
        def _(gpsimd):
            for h in range(NH):
                if h >= 2:
                    gpsimd.wait_ge(sem("mm3"), h - 1)
                for half in range(2):
                    src = Wq.ap().rearrange("(k p) m -> p k m", p=128)[:, half * 16:(half + 1) * 16, h * 128:(h + 1) * 128]
                    gpsimd.dma_start(out=wqb[h % 2][:, half * 16:(half + 1) * 16, :], in_=src).then_inc(sem("wq3"), 16)

        @block.tensor
        def _(tensor):
            for h in range(NH):
                tensor.wait_ge(sem("wq3"), 32 * (h + 1))
                if h >= 2:
                    tensor.wait_ge(sem("rp3"), 20 * (h - 2) + 18)
                for k in range(KC):
                    for s in range(4):
                        inst = tensor.matmul(
                            ps_q[h % 2][:, s * 512:(s + 1) * 512], wqb[h % 2][:, k, :],
                            actT[:, k * T + s * 512: k * T + (s + 1) * 512],
                            start=(k == 0), stop=(k == KC - 1),
                        )
                        if k == KC - 1 and s == 3:
                            inst.then_inc(sem("mm3"), 1)

        @block.vector
        def _(vector):
            for h in range(NH):
                i = h % 2
                vector.wait_ge(sem("mm3"), h + 1)
                if h >= 2:
                    vector.wait_ge(sem("qw3"), 16 * (h - 1))
                base = 20 * h
                for sl in range(4):
                    L = sl * 512
                    R = (sl + 1) * 512
                    bb = base + 5 * sl
                    ser(vector, "rp3", bb + 0, lambda i=i, L=L, R=R: vector.tensor_tensor(
                        out=rtt[0:64, :], in0=ps_q[i][64:128, L:R], in1=sinT[:, L:R], op=ALU.mult))
                    ser(vector, "rp3", bb + 1, lambda i=i, L=L, R=R: vector.tensor_tensor(
                        out=rtt[64:128, :], in0=ps_q[i][0:64, L:R], in1=sinT[:, L:R], op=ALU.mult))
                    ser(vector, "rp3", bb + 2, lambda i=i, L=L, R=R: vector.tensor_tensor(
                        out=rkc[:], in0=ps_q[i][:, L:R], in1=cos2[:, L:R], op=ALU.mult))
                    ser(vector, "rp3", bb + 3, lambda i=i, L=L, R=R: vector.tensor_tensor(
                        out=qro[i][0:64, L:R], in0=rkc[0:64, :], in1=rtt[0:64, :], op=ALU.subtract))
                    ser(vector, "rp3", bb + 4, lambda i=i, L=L, R=R: vector.tensor_tensor(
                        out=qro[i][64:128, L:R], in0=rkc[64:128, :], in1=rtt[64:128, :], op=ALU.add))

        @block.sync
        def _(sync):
            for h in range(NH):
                sync.wait_ge(sem("rp3"), 20 * (h + 1))
                sync.dma_start(out=qT_s.ap()[h], in_=qro[h % 2][:]).then_inc(sem("qw3"), 16)

        es_p3.close()


    es_a.close()

    if upto == "p3":
        _emit_debug_tail(nc, sem, actT, out_t, dbg, qT_s, kT_s, vT_s, x2_s)
        return nc

    chunks = []
    dpref = 0
    for h in range(NH):
        for s in range(NSLAB):
            nch = 4 * s + 4
            for c in range(nch):
                diag = (c >= 4 * s)
                if diag:
                    dpref += 1
                chunks.append(dict(h=h, s=s, c=c, first=(c == 0), last=(c == nch - 1),
                                   diag=diag, idx=len(chunks), dpref=dpref,
                                   si=h * NSLAB + s))

    es_p4 = ExitStack()
    qTb = [es_p4.enter_context(nc.sbuf_tensor(f"qTb{i}", [128, T], BF16)) for i in range(3)]
    kTb = [es_p4.enter_context(nc.sbuf_tensor(f"kTb{i}", [128, T], BF16)) for i in range(2)]
    Vb = [es_p4.enter_context(nc.sbuf_tensor(f"Vb{i}", [128, NT, 128], BF16)) for i in range(2)]
    qsq = es_p4.enter_context(nc.sbuf_tensor("qsq", [128, T], BF16))
    ksq = es_p4.enter_context(nc.sbuf_tensor("ksq", [128, T], BF16))
    NPT = 6
    pTb = [es_p4.enter_context(nc.sbuf_tensor(f"pTb{i}", [128, SW], BF16)) for i in range(NPT)]
    rdn = [es_p4.enter_context(nc.sbuf_tensor(f"rdn{i}", [128, SW], F32)) for i in range(2)]
    lnd = [es_p4.enter_context(nc.sbuf_tensor(f"lnd{i}", [128, SW], F32)) for i in range(2)]
    mask_sb = es_p4.enter_context(nc.sbuf_tensor("mask_sb", [128, 4, SW], BF16))
    nbh = es_p4.enter_context(nc.sbuf_tensor("nbh", [128, NH], F32))
    nk_sb = es_p4.enter_context(nc.sbuf_tensor("nk_sb", [1, NKV], F32))
    sc4 = es_p4.enter_context(nc.sbuf_tensor("sc4", [1, 8], F32))
    ps_sT = [es_p4.enter_context(nc.psum_tensor(f"ps_sT{i}", [128, SW], F32)) for i in range(3)]
    ps_ctx = [es_p4.enter_context(nc.psum_tensor(f"ps_ctx{i}", [128, SW], F32)) for i in range(2)]
    ps_db = [es_p4.enter_context(nc.psum_tensor(f"ps_db{i}", [128, SW], F32)) for i in range(2)]
    ps_ms = es_p4.enter_context(nc.psum_tensor("ps_ms", [128, SW], F32))

    def prologue_ops(hp):
        ops = []
        kv = hp // 4
        if hp % 4 == 0:
            def f_ksq(e, kv=kv):
                e.wait_ge(sem("kld4"), 16 * (kv + 2))
                return e.activation(out=ksq[:], in_=kTb[kv % 2][:], func=AF.Square)
            ops.append(("scalar", f_ksq))
            for s4 in range(4):
                ops.append(("tensor", lambda e, s4=s4: e.matmul(
                    ps_ms[0:1, :], onesb[:, 0:1], ksq[:, s4 * 512:(s4 + 1) * 512],
                    start=True, stop=True)))
                if s4 == 0:
                    ops.append(("vector", lambda e: e.reduce_max(
                        out=sc4[0:1, 0:1], in_=ps_ms[0:1, :], axis=AX.X)))
                else:
                    ops.append(("vector", lambda e: e.reduce_max(
                        out=sc4[0:1, 7:8], in_=ps_ms[0:1, :], axis=AX.X)))
                    ops.append(("vector", lambda e: e.tensor_tensor(
                        out=sc4[0:1, 0:1], in0=sc4[0:1, 0:1], in1=sc4[0:1, 7:8], op=ALU.max)))
            ops.append(("vector", lambda e, kv=kv: e.tensor_copy(
                out=nk_sb[0:1, kv:kv + 1], in_=sc4[0:1, 0:1])))

        def f_qsq(e, hp=hp):
            e.wait_ge(sem("qld4"), 16 * (hp + 1))
            return e.activation(out=qsq[:], in_=qTb[hp % 3][:], func=AF.Square)
        ops.append(("scalar", f_qsq))
        for s4 in range(4):
            ops.append(("tensor", lambda e, s4=s4: e.matmul(
                ps_ms[0:1, :], onesb[:, 0:1], qsq[:, s4 * 512:(s4 + 1) * 512],
                start=True, stop=True)))
            if s4 == 0:
                ops.append(("vector", lambda e: e.reduce_max(
                    out=sc4[0:1, 0:1], in_=ps_ms[0:1, :], axis=AX.X)))
            else:
                ops.append(("vector", lambda e: e.reduce_max(
                    out=sc4[0:1, 7:8], in_=ps_ms[0:1, :], axis=AX.X)))
                ops.append(("vector", lambda e: e.tensor_tensor(
                    out=sc4[0:1, 0:1], in0=sc4[0:1, 0:1], in1=sc4[0:1, 7:8], op=ALU.max)))
        ops.append(("vector", lambda e, kv=kv: e.tensor_tensor(
            out=sc4[0:1, 1:2], in0=sc4[0:1, 0:1], in1=nk_sb[0:1, kv:kv + 1], op=ALU.mult)))
        ops.append(("vector", lambda e: e.tensor_scalar_mul(
            out=sc4[0:1, 1:2], in0=sc4[0:1, 1:2], scalar1=1.0 / 128.0)))
        ops.append(("scalar", lambda e: e.activation(
            out=sc4[0:1, 2:3], in_=sc4[0:1, 1:2], func=AF.Ln)))
        ops.append(("scalar", lambda e: e.activation(
            out=sc4[0:1, 3:4], in_=sc4[0:1, 2:3], func=AF.Exp, scale=0.5)))
        ops.append(("vector", lambda e: e.tensor_scalar(
            out=sc4[0:1, 4:5], in0=sc4[0:1, 3:4], scalar1=-1.0, scalar2=80.0,
            op0=ALU.mult, op1=ALU.add)))
        ops.append(("vector", lambda e: e.tensor_single_scalar(
            out=sc4[0:1, 5:6], in_=sc4[0:1, 4:5], scalar=0.0, op=ALU.min)))
        ops.append(("tensor", lambda e: e.matmul(
            ps_ms[0:128, 0:1], onesf[:], sc4[0:1, 5:6], start=True, stop=True)))
        ops.append(("scalar", lambda e, hp=hp: e.copy(
            out=nbh[:, hp:hp + 1], in_=ps_ms[0:128, 0:1])))
        return ops

    PR_LISTS = [prologue_ops(h) for h in range(NH)]
    PR_BASE = [0]
    for h in range(NH):
        PR_BASE.append(PR_BASE[-1] + len(PR_LISTS[h]))

    def emit_prologue(engname, eng, hp):
        base = PR_BASE[hp]
        for j, (en, fn) in enumerate(PR_LISTS[hp]):
            if en != engname:
                continue
            if base + j > 0:
                eng.wait_ge(sem("pr4"), base + j)
            fn(eng).then_inc(sem("pr4"), 1)

    with nc.Block("p4") as block:

        @block.sync
        def _(sync):
            sync.dma_start(out=mask_sb[:], in_=mask_c.ap().rearrange("(r p) q -> p r q", p=128)).then_inc(sem("kld4"), 16)
            sync.wait_ge(sem("wq3"), 32 * NH)
            sync.wait_ge(sem("kw2"), 16 * 1)
            sync.dma_start(out=kTb[0][:], in_=kT_s.ap()[0]).then_inc(sem("kld4"), 16)
            sync.wait_ge(sem("vw2"), 16 * 1)
            for c in range(NT):
                sync.dma_start(out=Vb[0][:, c, :], in_=vT_s.ap()[0][:, c * 128:(c + 1) * 128], transpose=True).then_inc(sem("vld4"), 16)
            for h in range(NH):
                if h >= 3:
                    sync.wait_ge(sem("hd4"), h - 2)
                sync.wait_ge(sem("qw3"), 16 * (h + 1))
                sync.dma_start(out=qTb[h % 3][:], in_=qT_s.ap()[h]).then_inc(sem("qld4"), 16)
                if h % 4 == 2 and h // 4 + 1 < NKV:
                    kvp = h // 4 + 1
                    sync.wait_ge(sem("hd4"), 4 * (kvp - 1))
                    sync.wait_ge(sem("kw2"), 16 * (kvp + 1))
                    sync.dma_start(out=kTb[kvp % 2][:], in_=kT_s.ap()[kvp]).then_inc(sem("kld4"), 16)
                    sync.wait_ge(sem("vw2"), 16 * (kvp + 1))
                    for c in range(NT):
                        sync.dma_start(out=Vb[kvp % 2][:, c, :], in_=vT_s.ap()[kvp][:, c * 128:(c + 1) * 128], transpose=True).then_inc(sem("vld4"), 16)

        @block.tensor
        def _(tensor):
            emit_prologue("tensor", tensor, 0)

            def emit_scores(ch):
                h, s, c, idx = ch["h"], ch["s"], ch["c"], ch["idx"]
                kv = h // 4
                if ch["first"] and s == 0:
                    tensor.wait_ge(sem("qld4"), 16 * (h + 1))
                    tensor.wait_ge(sem("kld4"), 16 * (kv + 2))
                if idx >= 3:
                    tensor.wait_ge(sem("exp4"), idx - 2)
                r0 = (c - 4 * s) * 128 if ch["diag"] else 0
                tensor.matmul(
                    ps_sT[idx % 3][:, r0:SW], kTb[kv % 2][:, c * 128:(c + 1) * 128],
                    qTb[h % 3][:, s * SW + r0:(s + 1) * SW], start=True, stop=True,
                ).then_inc(sem("sT4"), 1)

            for j in range(3):
                emit_scores(chunks[j])
            for ch in chunks:
                h, s, c, idx, si = ch["h"], ch["s"], ch["c"], ch["idx"], ch["si"]
                kv = h // 4
                if idx + 3 < len(chunks):
                    emit_scores(chunks[idx + 3])
                if ch["first"] and s == 2 and h + 1 < NH:
                    emit_prologue("tensor", tensor, h + 1)
                if ch["first"] and s == 0:
                    tensor.wait_ge(sem("vld4"), 16 * NT * (kv + 1))
                tensor.wait_ge(sem("exp4"), idx + 1)
                if ch["first"]:
                    if si >= 2:
                        tensor.wait_ge(sem("ln4"), si - 1)
                        tensor.wait_ge(sem("ce4"), si - 1)
                r0 = (c - 4 * s) * 128 if ch["diag"] else 0
                r1 = r0 + 128 if ch["diag"] else 0
                split = ch["diag"] and r1 < SW and not ch["first"]
                if split:
                    tensor.matmul(
                        ps_db[si % 2][:, r1:SW], onesq[:], pTb[idx % NPT][:, r1:SW],
                        start=False, stop=False,
                    )
                    tensor.matmul(
                        ps_ctx[si % 2][:, r1:SW], Vb[kv % 2][:, c, :], pTb[idx % NPT][:, r1:SW],
                        start=False, stop=False,
                    )
                if ch["diag"]:
                    tensor.wait_ge(sem("msk4"), ch["dpref"])
                    lo, hi = (r0, r1) if split else (r0, SW)
                else:
                    lo, hi = 0, SW
                tensor.matmul(
                    ps_db[si % 2][:, lo:hi], onesq[:], pTb[idx % NPT][:, lo:hi],
                    start=ch["first"], stop=ch["last"],
                )
                tensor.matmul(
                    ps_ctx[si % 2][:, lo:hi], Vb[kv % 2][:, c, :], pTb[idx % NPT][:, lo:hi],
                    start=ch["first"], stop=ch["last"],
                ).then_inc(sem("pec4"), 1)
                if ch["last"] and s == NSLAB - 1:
                    tensor.nop().then_inc(sem("hd4"), 1)

        @block.scalar
        def _(scalar):
            emit_prologue("scalar", scalar, 0)

            def slab_epilogue(si, idx_last):
                scalar.wait_ge(sem("pec4"), idx_last + 1)
                scalar.activation(
                    out=lnd[si % 2][:], in_=ps_db[si % 2][:], func=AF.Ln,
                ).then_inc(sem("ln4"), 1)
                scalar.wait_ge(sem("ln4"), si + 1)
                if si >= 2:
                    scalar.wait_ge(sem("ce4"), si - 1)
                scalar.activation(
                    out=rdn[si % 2][:], in_=lnd[si % 2][:], func=AF.Exp, scale=-1.0,
                ).then_inc(sem("rd4"), 1)

            pend = []
            for ch in chunks:
                h, s, c, idx, si = ch["h"], ch["s"], ch["c"], ch["idx"], ch["si"]
                if ch["first"] and s == 0:
                    scalar.wait_ge(sem("pr4"), PR_BASE[h + 1])
                scalar.wait_ge(sem("sT4"), idx + 1)
                if idx >= NPT - 1:
                    scalar.wait_ge(sem("pec4"), idx - (NPT - 1))
                r0 = (c - 4 * s) * 128 if ch["diag"] else 0
                scalar.activation(
                    out=pTb[idx % NPT][:, r0:SW], in_=ps_sT[idx % 3][:, r0:SW], func=AF.Exp,
                    scale=ISQ, bias=nbh[:, h:h + 1],
                ).then_inc(sem("exp4"), 1)
                if pend and idx - pend[0][1] >= 1:
                    psi, pidx = pend.pop(0)
                    slab_epilogue(psi, pidx)
                if ch["first"] and s == 2 and h + 1 < NH:
                    while pend:
                        psi, pidx = pend.pop(0)
                        slab_epilogue(psi, pidx)
                    emit_prologue("scalar", scalar, h + 1)
                if ch["last"]:
                    pend.append((si, idx))
            while pend:
                psi, pidx = pend.pop(0)
                slab_epilogue(psi, pidx)

        @block.vector
        def _(vector):
            emit_prologue("vector", vector, 0)
            for ch in chunks:
                h, s, c, idx, si = ch["h"], ch["s"], ch["c"], ch["idx"], ch["si"]
                if ch["first"] and s == 2 and h + 1 < NH:
                    emit_prologue("vector", vector, h + 1)
                if ch["diag"]:
                    r0 = (c - 4 * s) * 128
                    vector.wait_ge(sem("exp4"), idx + 1)
                    vector.tensor_tensor(
                        out=pTb[idx % NPT][:, r0:r0 + 128], in0=pTb[idx % NPT][:, r0:r0 + 128],
                        in1=mask_sb[:, 0, 0:128], op=ALU.mult,
                    ).then_inc(sem("msk4"), 1)
                if ch["last"]:
                    vector.wait_ge(sem("rd4"), si + 1)
                    vector.tensor_tensor(
                        out=actT[:, h * T + s * SW: h * T + (s + 1) * SW],
                        in0=ps_ctx[si % 2][:], in1=rdn[si % 2][:], op=ALU.mult,
                    ).then_inc(sem("ce4"), 1)

    es_p4.close()

    if upto == "p4":
        _emit_debug_tail(nc, sem, actT, out_t, dbg, qT_s, kT_s, vT_s, x2_s)
        return nc

    NOS5 = 8
    NG5 = 4
    EP5 = 6
    with nc.Block("p5") as block:
        es_p5 = ExitStack()
        wob = [es_p5.enter_context(nc.sbuf_tensor(f"wob{i}", [128, KC, 512], BF16)) for i in range(2)]
        xcol = [es_p5.enter_context(nc.sbuf_tensor(f"xcol{i}", [128, 512], F32)) for i in range(2)]
        x2col = [es_p5.enter_context(nc.sbuf_tensor(f"x2col{i}", [128, 512], F32)) for i in range(2)]
        sqcol = es_p5.enter_context(nc.sbuf_tensor("sqcol", [128, 512], F32))
        red5 = es_p5.enter_context(nc.sbuf_tensor("red5t", [128, 2], F32))
        ps5 = [es_p5.enter_context(nc.psum_tensor(f"ps5_{i}", [128, 512], F32)) for i in range(8)]

        @block.vector
        def _(vector):
            vector.memset(s1acc[:], 0.0).then_inc(sem("ep5"), 1)
            vector.memset(s2acc[:], 0.0).then_inc(sem("ep5"), 1)

        @block.gpsimd
        def _(gpsimd):
            for os in range(NOS5):
                if os >= 2:
                    gpsimd.wait_ge(sem("mm5"), (os - 1) * NG5)
                for qr in range(4):
                    src = Wo.ap().rearrange("(k p) m -> p k m", p=128)[:, qr * 8:(qr + 1) * 8, os * 512:(os + 1) * 512]
                    gpsimd.dma_start(out=wob[os % 2][:, qr * 8:(qr + 1) * 8, :], in_=src).then_inc(sem("wo5"), 16)

        @block.sync
        def _(sync):
            n = 0
            for os in range(NOS5):
                for g in range(NG5):
                    for tt in range(4):
                        t_g = g * 4 + tt
                        n += 1
                        if n >= 3:
                            sync.wait_ge(sem("ep5"), 2 + (n - 2) * EP5)
                        sync.dma_start(
                            out=xcol[n % 2][:],
                            in_=x_in.ap()[t_g * 128:(t_g + 1) * 128, os * 512:(os + 1) * 512],
                        ).then_inc(sem("xc5"), 16)

        @block.tensor
        def _(tensor):
            for os in range(NOS5):
                tensor.wait_ge(sem("wo5"), 64 * (os + 1))
                for g in range(NG5):
                    gi = os * NG5 + g
                    bank = (gi % 2) * 4
                    if gi >= 2:
                        tensor.wait_ge(sem("ep5"), 2 + ((gi - 2) * 4 + 4) * EP5)
                    for k in range(KC):
                        for tt in range(4):
                            t_g = g * 4 + tt
                            inst = tensor.matmul(
                                ps5[bank + tt][:],
                                actT[:, k * T + t_g * 128: k * T + (t_g + 1) * 128],
                                wob[os % 2][:, k, :],
                                start=(k == 0), stop=(k == KC - 1),
                            )
                            if k == KC - 1 and tt == 3:
                                inst.then_inc(sem("mm5"), 1)

        @block.vector
        def _(vector):
            n = 0
            for os in range(NOS5):
                for g in range(NG5):
                    gi = os * NG5 + g
                    vector.wait_ge(sem("mm5"), gi + 1)
                    bank = (gi % 2) * 4
                    for tt in range(4):
                        t_g = g * 4 + tt
                        n += 1
                        base = 2 + (n - 1) * EP5
                        vector.wait_ge(sem("xc5"), 16 * n)
                        if n >= 3:
                            vector.wait_ge(sem("x2w5"), 16 * (n - 2))
                        ser(vector, "ep5", base + 0, lambda bank=bank, tt=tt, n=n: vector.tensor_tensor(
                            out=x2col[n % 2][:], in0=ps5[bank + tt][:], in1=xcol[n % 2][:], op=ALU.add))
                        ser(vector, "ep5", base + 1, lambda n=n: vector.reduce_sum(
                            out=red5[:, 0:1], in_=x2col[n % 2][:], axis=AX.X))
                        ser(vector, "ep5", base + 3, lambda t_g=t_g: vector.tensor_tensor(
                            out=s1acc[:, t_g:t_g + 1], in0=s1acc[:, t_g:t_g + 1], in1=red5[:, 0:1], op=ALU.add))
                        ser(vector, "ep5", base + 4, lambda t_g=t_g: vector.tensor_tensor(
                            out=s2acc[:, t_g:t_g + 1], in0=s2acc[:, t_g:t_g + 1], in1=red5[:, 1:2], op=ALU.add))

        @block.scalar
        def _(scalar):
            n = 0
            for os in range(NOS5):
                for g in range(NG5):
                    for tt in range(4):
                        t_g = g * 4 + tt
                        n += 1
                        base = 2 + (n - 1) * EP5
                        ser(scalar, "ep5", base + 2, lambda n=n: scalar.activation(
                            out=sqcol[:], in_=x2col[n % 2][:], func=AF.Square,
                            accum_out=red5[:, 1:2]))
                        scalar.wait_ge(sem("ep5"), base + 5)
                        scalar.dma_start(
                            out=x2_s.ap()[t_g * 128:(t_g + 1) * 128, os * 512:(os + 1) * 512],
                            in_=x2col[n % 2][:],
                        ).then_inc(sem("x2w5"), 16)
                        scalar.nop().then_inc(sem("ep5"), 1)

        es_p5.close()

    if upto == "p5":
        _emit_debug_tail(nc, sem, actT, out_t, dbg, qT_s, kT_s, vT_s, x2_s)
        return nc

    es_c = ExitStack()
    AeffT = es_c.enter_context(nc.sbuf_tensor("AeffT", [128, KC * 16], BF16))
    BeffT = es_c.enter_context(nc.sbuf_tensor("BeffT", [16, KC * 128], BF16))
    tT_sb = es_c.enter_context(nc.sbuf_tensor("tT_sb", [16, T], BF16))

    with nc.Block("p6") as block:
        es_p6 = ExitStack()
        x2r = [es_p6.enter_context(nc.sbuf_tensor("x2r0", [128, D], F32))] * 2
        h2b = [es_p6.enter_context(nc.sbuf_tensor("h2b0", [128, D], BF16))] * 2
        Ast = es_p6.enter_context(nc.sbuf_tensor("Ast", [16, 3 * 1024], F32))
        aeff_c = es_p6.enter_context(nc.sbuf_tensor("aeff_c", [16, 1024], F32))
        aeffb = es_p6.enter_context(nc.sbuf_tensor("aeffb", [16, D], BF16))
        B_sb = es_p6.enter_context(nc.sbuf_tensor("B_sb", [128, 3, KC, 16], F32))
        beffb = es_p6.enter_context(nc.sbuf_tensor("beffb", [128, 16], BF16))
        mv_sb = es_p6.enter_context(nc.sbuf_tensor("mv_sb", [128, 3], F32))
        pt6 = [es_p6.enter_context(nc.psum_tensor(f"pt6_{i}", [128, 512], BF16)) for i in range(2)]
        ps6m = es_p6.enter_context(nc.psum_tensor("ps6m", [128, 512], F32))
        pt6b = es_p6.enter_context(nc.psum_tensor("pt6b", [128, 512], BF16))

        @block.vector
        def _(vector):
            ser(vector, "z6", 0, lambda: vector.tensor_scalar_mul(out=s1acc[:], in0=s1acc[:], scalar1=1.0 / D))
            ser(vector, "z6", 1, lambda: vector.tensor_scalar_mul(out=s2acc[:], in0=s2acc[:], scalar1=1.0 / D))
            ser(vector, "z6", 2, lambda: vector.tensor_tensor(out=rstd2[:], in0=s1acc[:], in1=s1acc[:], op=ALU.mult))
            ser(vector, "z6", 3, lambda: vector.tensor_tensor(out=rstd2[:], in0=rstd2[:], in1=s2acc[:], op=ALU.subtract))
            ser(vector, "z6", 4, lambda: vector.tensor_scalar(
                out=rstd2[:], in0=rstd2[:], scalar1=-1.0, scalar2=EPS, op0=ALU.mult, op1=ALU.add))
            ser(vector, "z6", 6, lambda: vector.reciprocal(out=rstd2[:], in_=negmr2[:]))
            ser(vector, "z6", 7, lambda: vector.scalar_tensor_tensor(
                out=negmr2[:], in0=s1acc[:], scalar=-1.0, in1=rstd2[:], op0=ALU.mult, op1=ALU.mult))

        @block.scalar
        def _(scalar):
            ser(scalar, "z6", 5, lambda: scalar.activation(
                out=negmr2[:], in_=rstd2[:], func=AF.Sqrt))

        @block.sync
        def _(sync):
            sync.wait_ge(sem("x2w5"), 16 * 128)
            for t in range(NT):
                if t >= 1:
                    sync.wait_ge(sem("ap6"), t)
                sync.dma_start(out=x2r[t % 2][:], in_=x2_s.ap()[t * 128:(t + 1) * 128, :]).then_inc(sem("xl6"), 16)

        @block.scalar
        def _(scalar):
            scalar.wait_ge(sem("z6"), 8)
            for t in range(NT):
                i = t % 2
                scalar.wait_ge(sem("xl6"), 16 * (t + 1))
                if t >= 1:
                    scalar.wait_ge(sem("tr6"), 8 * t)
                scalar.activation(
                    out=h2b[i][:], in_=x2r[i][:], func=AF.Identity,
                    scale=rstd2[:, t:t + 1], bias=negmr2[:, t:t + 1],
                ).then_inc(sem("ap6"), 1)
                for g in range(8):
                    gi = t * 8 + g
                    scalar.wait_ge(sem("tr6"), gi + 1)
                    dst = actT.ap().rearrange("p (c q) -> p c q", q=T)[:, 4 * g:4 * g + 4, t * 128:(t + 1) * 128]
                    scalar.copy(out=dst, in_=pt6[g % 2].ap().rearrange("p (c q) -> p c q", q=128)).then_inc(sem("ev6"), 1)

        @block.tensor
        def _(tensor):
            for t in range(NT):
                i = t % 2
                tensor.wait_ge(sem("ap6"), t + 1)
                for g in range(8):
                    gi = t * 8 + g
                    if gi >= 2:
                        tensor.wait_ge(sem("ev6"), gi - 1)
                    for j in range(4):
                        c = 4 * g + j
                        inst = tensor.transpose(
                            pt6[g % 2][:, j * 128:(j + 1) * 128],
                            h2b[i][:, c * 128:(c + 1) * 128],
                            idt[:],
                        )
                        if j == 3:
                            inst.then_inc(sem("tr6"), 1)

        lops = []
        cast_slot = {}

        def f_bcast(e):
            e.wait_ge(sem("z0"), 25)
            return e.matmul(ps6m[:, 0:3], onesf[:], mrow[:], start=True, stop=True)
        lops.append(("tensor", f_bcast))
        lops.append(("scalar", lambda e: e.copy(out=mv_sb[:], in_=ps6m[:, 0:3])))
        for cc in range(4):
            def f_a0(e, cc=cc):
                e.wait_ge(sem("g6a"), 48 * (cc + 1))
                return e.tensor_scalar(out=aeff_c[0:16, :], in0=Ast[0:16, 0:1024],
                                       scalar1=mv_sb[0:16, 0:1], scalar2=None, op0=ALU.mult)
            lops.append(("vector", f_a0))
            lops.append(("vector", lambda e: e.scalar_tensor_tensor(
                out=aeff_c[0:16, :], in0=Ast[0:16, 1024:2048], scalar=mv_sb[0:16, 1:2],
                in1=aeff_c[0:16, :], op0=ALU.mult, op1=ALU.add)))
            lops.append(("vector", lambda e: e.scalar_tensor_tensor(
                out=aeff_c[0:16, :], in0=Ast[0:16, 2048:3072], scalar=mv_sb[0:16, 2:3],
                in1=aeff_c[0:16, :], op0=ALU.mult, op1=ALU.add)))
            lops.append(("vector", lambda e, cc=cc: e.tensor_copy(
                out=aeffb[0:16, cc * 1024:(cc + 1) * 1024], in_=aeff_c[0:16, :])))
            cast_slot[cc] = len(lops) - 1
        for c in range(KC):
            lops.append(("tensor", lambda e, c=c: e.transpose(
                pt6b[:, c * 16:(c + 1) * 16], aeffb[0:16, c * 128:(c + 1) * 128], idt[0:16, 0:16])))
        lops.append(("scalar", lambda e: e.copy(out=AeffT[:], in_=pt6b[:, 0:512])))
        for c in range(KC):
            def f_b0(e, c=c):
                e.wait_ge(sem("g6"), 48)
                return e.tensor_scalar(out=beffb[:], in0=B_sb[:, 0, c, :],
                                       scalar1=mv_sb[:, 0:1], scalar2=None, op0=ALU.mult)
            lops.append(("vector", f_b0))
            lops.append(("vector", lambda e, c=c: e.scalar_tensor_tensor(
                out=beffb[:], in0=B_sb[:, 1, c, :], scalar=mv_sb[:, 1:2], in1=beffb[:],
                op0=ALU.mult, op1=ALU.add)))
            lops.append(("vector", lambda e, c=c: e.scalar_tensor_tensor(
                out=beffb[:], in0=B_sb[:, 2, c, :], scalar=mv_sb[:, 2:3], in1=beffb[:],
                op0=ALU.mult, op1=ALU.add)))
            lops.append(("tensor", lambda e, c=c: e.transpose(
                pt6b[0:16, (c % 4) * 128:((c % 4) + 1) * 128], beffb[:], idt[:])))
            if c % 4 == 3:
                lops.append(("scalar", lambda e, c=c: e.copy(
                    out=BeffT[0:16, (c - 3) * 128:(c + 1) * 128], in_=pt6b[0:16, 0:512])))
        run_oplist(block, "z6b", lops)

        @block.gpsimd
        def _(gpsimd):
            for a in range(3):
                gpsimd.dma_start(
                    out=B_sb[:, a, :, :],
                    in_=lB.ap()[a].rearrange("(c p) r -> p c r", p=128),
                ).then_inc(sem("g6"), 16)
            for cc in range(4):
                if cc >= 1:
                    gpsimd.wait_ge(sem("z6b"), cast_slot[cc - 1] + 1)
                for a in range(3):
                    gpsimd.dma_start(
                        out=Ast[0:16, a * 1024:(a + 1) * 1024],
                        in_=lA.ap()[a][:, cc * 1024:(cc + 1) * 1024],
                    ).then_inc(sem("g6a"), 16)

        es_p6.close()

    if upto == "p6":
        _emit_debug_tail(nc, sem, actT, out_t, dbg, qT_s, kT_s, vT_s, x2_s)
        return nc

    with nc.Block("p7") as block:
        es_p7 = ExitStack()
        wfth = [es_p7.enter_context(nc.sbuf_tensor(f"wfth{i}", [128, 16, 512], BF16)) for i in range(3)]
        bias_sb = es_p7.enter_context(nc.sbuf_tensor("bias_sb", [1, D], BF16))
        onesfb = es_p7.enter_context(nc.sbuf_tensor("onesfb", [1, 128], BF16))
        x2c7 = [es_p7.enter_context(nc.sbuf_tensor(f"x2c7_{i}", [128, 512], F32)) for i in range(2)]
        outc = [es_p7.enter_context(nc.sbuf_tensor(f"outc{i}", [128, 512], F32)) for i in range(2)]
        ps7 = [es_p7.enter_context(nc.psum_tensor(f"ps7_{i}", [128, 512], F32)) for i in range(8)]

        @block.scalar
        def _(scalar):
            for ts in range(4):
                scalar.wait_ge(sem("mm7"), ts + 1)
                scalar.copy(out=tT_sb[0:16, ts * 512:(ts + 1) * 512], in_=ps7[0][0:16, :]).then_inc(sem("tT7"), 1)

        @block.gpsimd
        def _(gpsimd):
            gpsimd.dma_start(out=bias_sb[:], in_=bias_t.ap()).then_inc(sem("l7"), 16)

        @block.scalar
        def _(scalar):
            scalar.dma_start(out=onesfb[:], in_=onesq_c.ap()[0:1, :]).then_inc(sem("l7"), 16)
            coords = []
            for os in range(8):
                for g in range(4):
                    for tt in range(4):
                        coords.append((g * 4 + tt, os))
            for n in (1, 2):
                t_g, os = coords[n - 1]
                scalar.dma_start(
                    out=x2c7[n % 2][:],
                    in_=x2_s.ap()[t_g * 128:(t_g + 1) * 128, os * 512:(os + 1) * 512],
                ).then_inc(sem("x2c7"), 16)
            for n_out in range(1, 129):
                if n_out + 2 <= 128:
                    n = n_out + 2
                    t_g, os = coords[n - 1]
                    scalar.wait_ge(sem("ev7"), n - 2)
                    scalar.dma_start(
                        out=x2c7[n % 2][:],
                        in_=x2_s.ap()[t_g * 128:(t_g + 1) * 128, os * 512:(os + 1) * 512],
                    ).then_inc(sem("x2c7"), 16)
                t_g, os = coords[n_out - 1]
                scalar.wait_ge(sem("ev7"), n_out)
                scalar.dma_start(
                    out=out_t.ap()[t_g * 128:(t_g + 1) * 128, os * 512:(os + 1) * 512],
                    in_=outc[n_out % 2][:],
                ).then_inc(sem("out7"), 16)

        @block.tensor
        def _(tensor):
            for ts in range(4):
                if ts >= 1:
                    tensor.wait_ge(sem("tT7"), ts)
                for k in range(KC):
                    inst = tensor.matmul(
                        ps7[0][0:16, :], AeffT[:, k * 16:(k + 1) * 16],
                        actT[:, k * T + ts * 512: k * T + (ts + 1) * 512],
                        start=(k == 0), stop=(k == KC - 1),
                    )
                    if k == KC - 1:
                        inst.then_inc(sem("mm7"), 1)
            for os in range(8):
                for g in range(4):
                    gi = os * 4 + g
                    bank = (gi % 2) * 4
                    if gi == 0:
                        tensor.wait_ge(sem("tT7"), 4)
                        tensor.wait_ge(sem("l7"), 32)
                    if gi >= 2:
                        tensor.wait_ge(sem("ev7"), (gi - 2) * 4 + 4)
                    for k in range(KC):
                        hh = os * 2 + k // 16
                        tensor.wait_ge(sem("wft7"), 16 * (os * KC + k + 1))
                        for tt in range(4):
                            t_g = g * 4 + tt
                            inst = tensor.matmul(
                                ps7[bank + tt][:],
                                actT[:, k * T + t_g * 128: k * T + (t_g + 1) * 128],
                                wfth[hh % 3][:, k % 16, :],
                                start=(k == 0), stop=False,
                            )
                            if g == 3 and tt == 3 and k % 16 == 15:
                                inst.then_inc(sem("osd7"), 1)
                    for tt in range(4):
                        t_g = g * 4 + tt
                        tensor.matmul(
                            ps7[bank + tt][:], onesfb[:], bias_sb[0:1, os * 512:(os + 1) * 512],
                            start=False, stop=False,
                        )
                        inst = tensor.matmul(
                            ps7[bank + tt][:], tT_sb[0:16, t_g * 128:(t_g + 1) * 128],
                            BeffT[0:16, os * 512:(os + 1) * 512],
                            start=False, stop=True,
                        )
                        if tt == 3:
                            inst.then_inc(sem("mm7"), 1)


        @block.sync
        def _(sync):
            sync.wait_ge(sem("wfw2"), 16 * 32)
            sync.wait_ge(sem("l7"), 32)
            for os in range(8):
                for half in range(2):
                    hh = os * 2 + half
                    if hh >= 3:
                        sync.wait_ge(sem("osd7"), hh - 2)
                    for kl in range(16):
                        k = half * 16 + kl
                        sync.dma_start(
                            out=wfth[hh % 3][:, kl, :],
                            in_=WfB.ap()[os * 512:(os + 1) * 512, k * 128:(k + 1) * 128],
                            transpose=True,
                        ).then_inc(sem("wft7"), 16)

        @block.vector
        def _(vector):
            n = 0
            for os in range(8):
                for g in range(4):
                    gi = os * 4 + g
                    bank = (gi % 2) * 4
                    vector.wait_ge(sem("mm7"), 4 + gi + 1)
                    for tt in range(4):
                        n += 1
                        vector.wait_ge(sem("x2c7"), 16 * n)
                        if n >= 3:
                            vector.wait_ge(sem("out7"), 16 * (n - 2))
                        vector.tensor_tensor(
                            out=outc[n % 2][:], in0=ps7[bank + tt][:], in1=x2c7[n % 2][:], op=ALU.add,
                        ).then_inc(sem("ev7"), 1)

        @block.gpsimd
        def _(gpsimd):
            gpsimd.wait_ge(sem("out7"), 16 * 128)

        es_p7.close()

    return nc


def _emit_debug_tail(nc, sem, actT, out_t, dbg, qT_s, kT_s, vT_s, x2_s):
    with nc.Block("dbgtail") as block:

        @block.gpsimd
        def _(gpsimd):
            dst = out_t.ap().rearrange("(a p) m -> p a m", p=128)
            gpsimd.dma_start(out=dst, in_=actT.ap().rearrange("p (a m) -> p a m", m=D)).then_inc(sem("dbg1"), 16)
            if dbg:
                for name, scr in [("qT", qT_s), ("kT", kT_s), ("vT", vT_s)]:
                    if name in dbg:
                        s = scr.ap()
                        d = dbg[name].ap()
                        gpsimd.dma_start(out=d, in_=s).then_inc(sem("dbg1"), 16)
                if "x2" in dbg:
                    gpsimd.dma_start(out=dbg["x2"].ap(), in_=x2_s.ap()).then_inc(sem("dbg1"), 16)
            n = 1 + (len(dbg) if dbg else 0)
            gpsimd.wait_ge(sem("dbg1"), 16 * n)


_NC_CACHE = {}


def _get_nc():
    if "nc" not in _NC_CACHE:
        _NC_CACHE["nc"] = build_nc()
    return _NC_CACHE["nc"]


def make_in_maps(inputs):
    x = np.ascontiguousarray(np.asarray(inputs["L0_attn_x"], dtype=np.float32))
    Wq = np.ascontiguousarray(np.asarray(inputs["L0_attn_Wq"], dtype=np.float32))
    Wk = np.ascontiguousarray(np.asarray(inputs["L0_attn_Wk"], dtype=np.float32))
    Wv = np.ascontiguousarray(np.asarray(inputs["L0_attn_Wv"], dtype=np.float32))
    Wo = np.ascontiguousarray(np.asarray(inputs["L0_attn_Wo"], dtype=np.float32))
    Wf = np.ascontiguousarray(np.asarray(inputs["L0_ffn_weight"], dtype=np.float32))
    fb = np.ascontiguousarray(np.asarray(inputs["L0_ffn_bias"], dtype=np.float32)).reshape(1, D)
    lAv = np.ascontiguousarray(np.asarray(inputs["L0_ffn_lora_A"], dtype=np.float32))
    lBv = np.ascontiguousarray(np.asarray(inputs["L0_ffn_lora_B"], dtype=np.float32))
    lsi = np.ascontiguousarray(np.asarray(inputs["L0_ffn_lora_start_indices"], dtype=np.int32)).reshape(1, 4)
    in_maps = []
    for b in range(B):
        bsel = np.zeros((1, 8), dtype=np.float32)
        bsel[0, b] = 1.0
        in_maps.append({
            "x": np.ascontiguousarray(x[b]), "Wq": Wq, "Wk": Wk, "Wv": Wv,
            "Wo": Wo, "Wf": Wf, "fbias": fb, "lA": lAv, "lB": lBv,
            "lsi": lsi, "bsel": bsel,
        })
    return in_maps


def kernel(**inputs):
    nc = _get_nc()
    in_maps = make_in_maps(inputs)
    res = run_bass_kernel_spmd(nc, in_maps, core_ids=list(range(B)))
    out = np.stack([res.results[b]["out"] for b in range(B)], axis=0)
    return out.astype(np.float32)


# version 16
# speedup vs baseline: 1.0482x; 1.0482x over previous
import numpy as np
import ml_dtypes
from contextlib import ExitStack

import concourse.bass as bass
import concourse.mybir as mybir
from concourse.bass_utils import run_bass_kernel_spmd

F32 = mybir.dt.float32
BF16 = mybir.dt.bfloat16
I32 = mybir.dt.int32
AF = mybir.ActivationFunctionType
ALU = mybir.AluOpType
AX = mybir.AxisListType
BF = ml_dtypes.bfloat16

B = 8
T = 2048
D = 4096
HD = 128
NH = 32
NKV = 8
KC = D // 128
NT = T // 128
SW = 512
NSLAB = T // SW
ISQ = float(1.0 / np.sqrt(128.0))
EPS = 1e-5
ROPE_BASE = 500000.0


def _host_tables():
    inv_freq = (1.0 / (ROPE_BASE ** (np.arange(0, HD, 2, dtype=np.float32) / HD))).astype(np.float32)
    t = np.arange(T, dtype=np.float32)
    freqs = np.einsum("i,j->ij", t, inv_freq).astype(np.float32)
    cos = np.cos(freqs).astype(np.float32)
    sin = np.sin(freqs).astype(np.float32)
    cosT = cos.T.copy()
    sinT = sin.T.copy()
    cos2 = np.concatenate([cosT, cosT], axis=0)
    masks = np.zeros((4, 128, SW), dtype=BF)
    for r in range(4):
        kl = np.arange(128)[:, None]
        ql = np.arange(SW)[None, :]
        masks[r] = ((r * 128 + kl) <= ql).astype(BF)
    return cos2, sinT, masks


def build_nc(upto="p7", debug=False):
    nc = bass.Bass("TRN2", target_bir_lowering=False)

    x_in = nc.dram_tensor("x", [T, D], F32, kind="ExternalInput")
    Wq = nc.dram_tensor("Wq", [D, D], F32, kind="ExternalInput")
    Wk = nc.dram_tensor("Wk", [D, NKV * HD], F32, kind="ExternalInput")
    Wv = nc.dram_tensor("Wv", [D, NKV * HD], F32, kind="ExternalInput")
    Wo = nc.dram_tensor("Wo", [D, D], F32, kind="ExternalInput")
    Wf = nc.dram_tensor("Wf", [D, D], F32, kind="ExternalInput")
    bias_t = nc.dram_tensor("fbias", [1, D], F32, kind="ExternalInput")
    lA = nc.dram_tensor("lA", [3, 16, D], F32, kind="ExternalInput")
    lB = nc.dram_tensor("lB", [3, D, 16], F32, kind="ExternalInput")
    lsi_t = nc.dram_tensor("lsi", [1, 4], I32, kind="ExternalInput")
    bsel_t = nc.dram_tensor("bsel", [1, 8], F32, kind="ExternalInput")
    out_t = nc.dram_tensor("out", [T, D], F32, kind="ExternalOutput")
    dbg = {}
    if debug:
        dbg["qT"] = nc.dram_tensor("d_qT", [NH, 128, T], F32, kind="ExternalOutput")
        dbg["kT"] = nc.dram_tensor("d_kT", [NKV, 128, T], F32, kind="ExternalOutput")
        dbg["vT"] = nc.dram_tensor("d_vT", [NKV, 128, T], F32, kind="ExternalOutput")
        dbg["x2"] = nc.dram_tensor("d_x2", [T, D], F32, kind="ExternalOutput")

    qT_s = nc.dram_tensor("qT_s", [NH, 128, T], BF16, kind="Internal")
    kT_s = nc.dram_tensor("kT_s", [NKV, 128, T], BF16, kind="Internal")
    vT_s = nc.dram_tensor("vT_s", [NKV, 128, T], BF16, kind="Internal")
    x2_s = nc.dram_tensor("x2_s", [T, D], F32, kind="Internal")
    WfB = nc.dram_tensor("WfB", [D, D], BF16, kind="Internal")

    cos2_np, sinT_np, masks_np = _host_tables()
    ident_c = nc.inline_tensor(np.eye(128).astype(BF), name="identc")
    iota8_c = nc.inline_tensor(np.arange(8, dtype=np.float32).reshape(1, 8), name="iota8c")
    onesb_c = nc.inline_tensor(np.ones((128, 1), dtype=BF), name="onesbc")
    onesq_c = nc.inline_tensor(np.ones((128, 128), dtype=BF), name="onesqc")
    onesf_c = nc.inline_tensor(np.ones((1, 128), dtype=np.float32), name="onesfc")
    cos2_c = nc.inline_tensor(cos2_np, name="cos2c")
    sinT_c = nc.inline_tensor(sinT_np, name="sinTc")
    mask_c = nc.inline_tensor(masks_np.reshape(4 * 128, SW), name="maskc")

    actT = nc.alloc_sbuf_tensor("actT", [128, KC * T], BF16)
    idt = nc.alloc_sbuf_tensor("idt", [128, 128], BF16)
    onesb = nc.alloc_sbuf_tensor("onesb", [128, 1], BF16)
    onesq = nc.alloc_sbuf_tensor("onesq", [128, 128], BF16)
    onesf = nc.alloc_sbuf_tensor("onesf", [1, 128], F32)
    iota8 = nc.alloc_sbuf_tensor("iota8", [1, 8], F32)
    bsel_sb = nc.alloc_sbuf_tensor("bsel_sb", [1, 8], F32)
    lsi_i = nc.alloc_sbuf_tensor("lsi_i", [1, 4], I32)
    lsi_f = nc.alloc_sbuf_tensor("lsi_f", [1, 4], F32)
    mrow = nc.alloc_sbuf_tensor("mrow", [1, 3], F32)
    rstd2 = nc.alloc_sbuf_tensor("rstd2", [128, NT], F32)
    negmr2 = nc.alloc_sbuf_tensor("negmr2", [128, NT], F32)
    s1acc = nc.alloc_sbuf_tensor("s1acc", [128, NT], F32)
    s2acc = nc.alloc_sbuf_tensor("s2acc", [128, NT], F32)

    SEM = {}

    def sem(name):
        if name not in SEM:
            SEM[name] = nc.alloc_semaphore(name)
        return SEM[name]

    def ser(eng, name, j, inst_fn):
        if j > 0:
            eng.wait_ge(sem(name), j)
        inst_fn().then_inc(sem(name), 1)
        return j + 1

    def run_oplist(block, name, oplist):
        for engname in ["sync", "gpsimd", "tensor", "vector", "scalar"]:
            ops = [(j, fn) for j, (en, fn) in enumerate(oplist) if en == engname]
            if not ops:
                continue

            def emit(e, ops=ops):
                for j, fn in ops:
                    if j > 0:
                        e.wait_ge(sem(name), j)
                    fn(e).then_inc(sem(name), 1)

            getattr(block, engname)(emit)

    es_a = ExitStack()
    cos2 = es_a.enter_context(nc.sbuf_tensor("cos2", [128, T], F32))
    sinT = es_a.enter_context(nc.sbuf_tensor("sinT", [64, T], F32))
    tmp8 = es_a.enter_context(nc.sbuf_tensor("tmp8", [1, 8 * 12], F32))
    rtt = es_a.enter_context(nc.sbuf_tensor("rtt", [128, 512], F32))
    rkc = es_a.enter_context(nc.sbuf_tensor("rkc", [128, 512], F32))

    def t8(i):
        return tmp8[0:1, i * 8:(i + 1) * 8]

    with nc.Block("p0") as block:

        @block.sync
        def _(sync):
            sync.dma_start(out=idt[:], in_=ident_c.ap()).then_inc(sem("l0"), 16)
            sync.dma_start(out=onesb[:], in_=onesb_c.ap()).then_inc(sem("l0"), 16)
            sync.dma_start(out=onesq[:], in_=onesq_c.ap()).then_inc(sem("l0"), 16)
            sync.dma_start(out=onesf[:], in_=onesf_c.ap()).then_inc(sem("l0"), 16)
            sync.dma_start(out=iota8[:], in_=iota8_c.ap()).then_inc(sem("l0"), 16)
            sync.dma_start(out=bsel_sb[:], in_=bsel_t.ap()).then_inc(sem("l0"), 16)
            sync.dma_start(out=lsi_i[:], in_=lsi_t.ap()).then_inc(sem("l0"), 16)
            sync.dma_start(out=cos2[:], in_=cos2_c.ap()).then_inc(sem("l0"), 16)
            sync.dma_start(out=sinT[:], in_=sinT_c.ap()).then_inc(sem("l0"), 16)

        @block.vector
        def _(vector):
            j = 0
            vector.wait_ge(sem("l0"), 16 * 7)
            j = ser(vector, "z0", j, lambda: vector.tensor_copy(out=lsi_f[:], in_=lsi_i[:]))
            for i in range(4):
                j = ser(vector, "z0", j, lambda i=i: vector.tensor_single_scalar(
                    out=t8(i), in_=iota8[:], scalar=lsi_f[0:1, i:i + 1], op=ALU.is_lt))
            j = ser(vector, "z0", j, lambda: vector.scalar_tensor_tensor(
                out=t8(4), in0=t8(0), scalar=-1.0, in1=t8(1), op0=ALU.mult, op1=ALU.mult))
            j = ser(vector, "z0", j, lambda: vector.tensor_tensor(out=t8(4), in0=t8(4), in1=t8(1), op=ALU.add))
            j = ser(vector, "z0", j, lambda: vector.tensor_scalar(
                out=t8(5), in0=t8(1), scalar1=-2.0, scalar2=2.0, op0=ALU.mult, op1=ALU.add))
            j = ser(vector, "z0", j, lambda: vector.tensor_tensor(out=t8(4), in0=t8(4), in1=t8(5), op=ALU.add))
            j = ser(vector, "z0", j, lambda: vector.tensor_scalar(
                out=t8(6), in0=t8(3), scalar1=-1.0, scalar2=4.0, op0=ALU.mult, op1=ALU.add))
            j = ser(vector, "z0", j, lambda: vector.tensor_tensor(out=t8(7), in0=t8(4), in1=t8(2), op=ALU.mult))
            j = ser(vector, "z0", j, lambda: vector.tensor_scalar(
                out=t8(8), in0=t8(2), scalar1=-1.0, scalar2=1.0, op0=ALU.mult, op1=ALU.add))
            j = ser(vector, "z0", j, lambda: vector.tensor_tensor(out=t8(8), in0=t8(8), in1=t8(6), op=ALU.mult))
            j = ser(vector, "z0", j, lambda: vector.tensor_tensor(out=t8(7), in0=t8(7), in1=t8(8), op=ALU.add))
            j = ser(vector, "z0", j, lambda: vector.tensor_single_scalar(
                out=t8(9), in_=t8(7), scalar=1.5, op=ALU.is_ge))
            j = ser(vector, "z0", j, lambda: vector.tensor_tensor(out=t8(9), in0=t8(9), in1=bsel_sb[:], op=ALU.mult))
            j = ser(vector, "z0", j, lambda: vector.tensor_scalar(
                out=t8(10), in0=t8(7), scalar1=-2.0, scalar2=0.0, op0=ALU.add, op1=ALU.max))
            j = ser(vector, "z0", j, lambda: vector.tensor_single_scalar(
                out=t8(10), in_=t8(10), scalar=2.0, op=ALU.min))
            for a in range(3):
                j = ser(vector, "z0", j, lambda a=a: vector.tensor_single_scalar(
                    out=t8(11), in_=t8(10), scalar=float(a), op=ALU.is_equal))
                j = ser(vector, "z0", j, lambda: vector.tensor_tensor(
                    out=t8(11), in0=t8(11), in1=t8(9), op=ALU.mult))
                j = ser(vector, "z0", j, lambda a=a: vector.reduce_sum(
                    out=mrow[0:1, a:a + 1], in_=t8(11), axis=AX.X))

    if upto == "p0":
        _emit_debug_tail(nc, sem, actT, out_t, dbg, qT_s, kT_s, vT_s, x2_s)
        return nc

    C1 = 10
    es_p1 = ExitStack()
    xt = [es_p1.enter_context(nc.sbuf_tensor(f"xt{i}", [128, D], F32)) for i in range(2)]
    xsq = [es_p1.enter_context(nc.sbuf_tensor("xsq0", [128, D], F32))] * 2
    h1b = [es_p1.enter_context(nc.sbuf_tensor("h1b0", [128, D], BF16))] * 2
    stt = es_p1.enter_context(nc.sbuf_tensor("sttiles", [128, 2, 8], F32))

    def stc(i, k):
        return stt[:, i, k:k + 1]

    with nc.Block("p1") as block:
        pt1 = [es_p1.enter_context(nc.psum_tensor(f"pt1_{i}", [128, 512], BF16)) for i in range(2)]

        @block.sync
        def _(sync):
            for t in range(NT):
                if t >= 2:
                    sync.wait_ge(sem("ap1"), t - 1)
                sync.dma_start(out=xt[t % 2][:], in_=x_in.ap()[t * 128:(t + 1) * 128, :]).then_inc(sem("xld1"), 16)

        @block.vector
        def _(vector):
            for t in range(NT):
                i = t % 2
                bt = 6 * t
                vector.wait_ge(sem("xld1"), 16 * (t + 1))
                if t >= 2:
                    vector.wait_ge(sem("ap1"), t - 1)
                vector.reduce_sum(out=stc(i, 0), in_=xt[i][:], axis=AX.X).then_inc(sem("st1"), 1)
                vector.wait_ge(sem("st1"), t + 1)
                if bt > 0:
                    vector.wait_ge(sem("sC1"), bt)
                vector.tensor_scalar_mul(out=stc(i, 2), in0=stc(i, 0), scalar1=1.0 / D).then_inc(sem("sC1"), 1)
                vector.wait_ge(sem("sA1"), t + 1)
                vector.wait_ge(sem("sC1"), bt + 1)
                vector.tensor_scalar_mul(out=stc(i, 3), in0=stc(i, 1), scalar1=1.0 / D).then_inc(sem("sC1"), 1)
                vector.wait_ge(sem("sC1"), bt + 2)
                vector.scalar_tensor_tensor(
                    out=stc(i, 4), in0=stc(i, 2), scalar=stc(i, 2), in1=stc(i, 3),
                    op0=ALU.mult, op1=ALU.subtract).then_inc(sem("sC1"), 1)
                vector.wait_ge(sem("sC1"), bt + 3)
                vector.tensor_scalar(
                    out=stc(i, 4), in0=stc(i, 4), scalar1=-1.0, scalar2=EPS,
                    op0=ALU.mult, op1=ALU.add).then_inc(sem("sC1"), 1)
                vector.wait_ge(sem("sD1"), t + 1)
                vector.reciprocal(out=stc(i, 6), in_=stc(i, 5)).then_inc(sem("sC1"), 1)
                vector.wait_ge(sem("sC1"), bt + 5)
                vector.scalar_tensor_tensor(
                    out=stc(i, 7), in0=stc(i, 2), scalar=-1.0, in1=stc(i, 6),
                    op0=ALU.mult, op1=ALU.mult).then_inc(sem("sC1"), 1)

        @block.scalar
        def _(scalar):
            for t in range(NT):
                i = t % 2
                bt = 6 * t
                scalar.wait_ge(sem("xld1"), 16 * (t + 1))
                if t >= 2:
                    scalar.wait_ge(sem("sC1"), 6 * (t - 2) + 2)
                scalar.activation(
                    out=xsq[i][:], in_=xt[i][:], func=AF.Square,
                    accum_out=stc(i, 1)).then_inc(sem("sA1"), 1)
                scalar.wait_ge(sem("sC1"), bt + 4)
                scalar.activation(
                    out=stc(i, 5), in_=stc(i, 4), func=AF.Sqrt).then_inc(sem("sD1"), 1)
                scalar.wait_ge(sem("sC1"), bt + 6)
                if t >= 1:
                    scalar.wait_ge(sem("tr1"), 8 * t)
                scalar.activation(
                    out=h1b[i][:], in_=xt[i][:], func=AF.Identity,
                    scale=stc(i, 6), bias=stc(i, 7),
                ).then_inc(sem("ap1"), 1)
                for g in range(8):
                    gi = t * 8 + g
                    scalar.wait_ge(sem("tr1"), gi + 1)
                    dst = actT.ap().rearrange("p (c q) -> p c q", q=T)[:, 4 * g:4 * g + 4, t * 128:(t + 1) * 128]
                    scalar.copy(out=dst, in_=pt1[g % 2].ap().rearrange("p (c q) -> p c q", q=128)).then_inc(sem("ev1"), 1)

        @block.tensor
        def _(tensor):
            for t in range(NT):
                i = t % 2
                tensor.wait_ge(sem("ap1"), t + 1)
                for g in range(8):
                    gi = t * 8 + g
                    if gi >= 2:
                        tensor.wait_ge(sem("ev1"), gi - 1)
                    for j in range(4):
                        c = 4 * g + j
                        inst = tensor.transpose(
                            pt1[g % 2][:, j * 128:(j + 1) * 128],
                            h1b[i][:, c * 128:(c + 1) * 128],
                            idt[:],
                        )
                        if j == 3:
                            inst.then_inc(sem("tr1"), 1)

        es_p1.close()

    if upto == "p1":
        _emit_debug_tail(nc, sem, actT, out_t, dbg, qT_s, kT_s, vT_s, x2_s)
        return nc

    UNITS2 = []
    for m in range(NKV):
        UNITS2.append(("k", m))
        UNITS2.append(("v", m))
    with nc.Block("p2") as block:
        es_p2 = ExitStack()
        wub = [es_p2.enter_context(nc.sbuf_tensor(f"wub{i}", [128, KC, 128], BF16)) for i in range(2)]
        kro = es_p2.enter_context(nc.sbuf_tensor("kro0", [128, T], BF16))
        vbs = es_p2.enter_context(nc.sbuf_tensor("vbs0", [128, T], BF16))
        wfc = es_p2.enter_context(nc.sbuf_tensor("wfc0", [128, D], BF16))
        ps_u = [es_p2.enter_context(nc.psum_tensor(f"ps_u{i}", [128, T], F32)) for i in range(2)]

        @block.gpsimd
        def _(gpsimd):
            for u, (typ, m) in enumerate(UNITS2):
                if u >= 2:
                    gpsimd.wait_ge(sem("mm2"), u - 1)
                W_src = {"k": Wk, "v": Wv}[typ]
                for half in range(2):
                    srcw = W_src.ap().rearrange("(k p) m -> p k m", p=128)[:, half * 16:(half + 1) * 16, m * 128:(m + 1) * 128]
                    gpsimd.dma_start(out=wub[u % 2][:, half * 16:(half + 1) * 16, :], in_=srcw).then_inc(sem("wk2"), 16)
                if typ == "v":
                    for j in range(2):
                        cc = m * 2 + j
                        for sub in range(2):
                            c = cc * 2 + sub
                            if c >= 1:
                                gpsimd.wait_ge(sem("wfw2"), 16 * c)
                            gpsimd.dma_start(out=wfc[:], in_=Wf.ap()[c * 128:(c + 1) * 128, :]).then_inc(sem("wfc2"), 16)

        @block.scalar
        def _(scalar):
            nv = 0
            for u, (typ, m) in enumerate(UNITS2):
                if typ == "v":
                    for j in range(4):
                        c = m * 4 + j
                        scalar.wait_ge(sem("wfc2"), 16 * (c + 1))
                        scalar.dma_start(out=WfB.ap()[c * 128:(c + 1) * 128, :], in_=wfc[:]).then_inc(sem("wfw2"), 16)
                    nv += 1
                    scalar.wait_ge(sem("mm2"), u + 1)
                    if nv >= 2:
                        scalar.wait_ge(sem("vw2"), 16 * (nv - 1))
                    scalar.activation(out=vbs[:], in_=ps_u[u % 2][:], func=AF.Copy).then_inc(sem("vc2"), 1)

        @block.tensor
        def _(tensor):
            for u, (typ, m) in enumerate(UNITS2):
                tensor.wait_ge(sem("wk2"), 32 * (u + 1))
                if u >= 2:
                    prev = u - 2
                    if prev % 2 == 0:
                        tensor.wait_ge(sem("rp2"), 20 * (prev // 2) + 18)
                    else:
                        tensor.wait_ge(sem("vc2"), (prev + 1) // 2)
                for k in range(KC):
                    for s4 in range(4):
                        inst = tensor.matmul(
                            ps_u[u % 2][:, s4 * 512:(s4 + 1) * 512], wub[u % 2][:, k, :],
                            actT[:, k * T + s4 * 512: k * T + (s4 + 1) * 512],
                            start=(k == 0), stop=(k == KC - 1),
                        )
                        if k == KC - 1 and s4 == 3:
                            inst.then_inc(sem("mm2"), 1)

        @block.vector
        def _(vector):
            nk = 0
            for u, (typ, m) in enumerate(UNITS2):
                if typ == "v":
                    continue
                nk += 1
                vector.wait_ge(sem("mm2"), u + 1)
                if nk >= 2:
                    vector.wait_ge(sem("kw2"), 16 * (nk - 1))
                base = 20 * (nk - 1)
                for sl in range(4):
                    L = sl * 512
                    R = (sl + 1) * 512
                    bb = base + 5 * sl
                    ser(vector, "rp2", bb + 0, lambda u=u, L=L, R=R: vector.tensor_tensor(
                        out=rtt[0:64, :], in0=ps_u[u % 2][64:128, L:R], in1=sinT[:, L:R], op=ALU.mult))
                    ser(vector, "rp2", bb + 1, lambda u=u, L=L, R=R: vector.tensor_tensor(
                        out=rtt[64:128, :], in0=ps_u[u % 2][0:64, L:R], in1=sinT[:, L:R], op=ALU.mult))
                    ser(vector, "rp2", bb + 2, lambda u=u, L=L, R=R: vector.tensor_tensor(
                        out=rkc[:], in0=ps_u[u % 2][:, L:R], in1=cos2[:, L:R], op=ALU.mult))
                    ser(vector, "rp2", bb + 3, lambda L=L, R=R: vector.tensor_tensor(
                        out=kro[0:64, L:R], in0=rkc[0:64, :], in1=rtt[0:64, :], op=ALU.subtract))
                    ser(vector, "rp2", bb + 4, lambda L=L, R=R: vector.tensor_tensor(
                        out=kro[64:128, L:R], in0=rkc[64:128, :], in1=rtt[64:128, :], op=ALU.add))

        @block.sync
        def _(sync):
            nk = 0
            nv = 0
            for u, (typ, m) in enumerate(UNITS2):
                if typ == "k":
                    nk += 1
                    sync.wait_ge(sem("rp2"), 20 * nk)
                    sync.dma_start(out=kT_s.ap()[m], in_=kro[:]).then_inc(sem("kw2"), 16)
                else:
                    nv += 1
                    sync.wait_ge(sem("vc2"), nv)
                    sync.dma_start(out=vT_s.ap()[m], in_=vbs[:]).then_inc(sem("vw2"), 16)

        es_p2.close()

    if upto == "p2":
        _emit_debug_tail(nc, sem, actT, out_t, dbg, qT_s, kT_s, vT_s, x2_s)
        return nc

    with nc.Block("p3") as block:
        es_p3 = ExitStack()
        wqb = [es_p3.enter_context(nc.sbuf_tensor(f"wqb{i}", [128, KC, 128], BF16)) for i in range(2)]
        qro = [es_p3.enter_context(nc.sbuf_tensor(f"qro{i}", [128, T], BF16)) for i in range(2)]
        ps_q = [es_p3.enter_context(nc.psum_tensor(f"ps_q{i}", [128, T], F32)) for i in range(2)]

        @block.gpsimd
        def _(gpsimd):
            for h in range(NH):
                if h >= 2:
                    gpsimd.wait_ge(sem("mm3"), h - 1)
                for half in range(2):
                    src = Wq.ap().rearrange("(k p) m -> p k m", p=128)[:, half * 16:(half + 1) * 16, h * 128:(h + 1) * 128]
                    gpsimd.dma_start(out=wqb[h % 2][:, half * 16:(half + 1) * 16, :], in_=src).then_inc(sem("wq3"), 16)

        @block.tensor
        def _(tensor):
            for h in range(NH):
                tensor.wait_ge(sem("wq3"), 32 * (h + 1))
                if h >= 2:
                    tensor.wait_ge(sem("rp3"), 20 * (h - 2) + 18)
                for k in range(KC):
                    for s in range(4):
                        inst = tensor.matmul(
                            ps_q[h % 2][:, s * 512:(s + 1) * 512], wqb[h % 2][:, k, :],
                            actT[:, k * T + s * 512: k * T + (s + 1) * 512],
                            start=(k == 0), stop=(k == KC - 1),
                        )
                        if k == KC - 1 and s == 3:
                            inst.then_inc(sem("mm3"), 1)

        @block.vector
        def _(vector):
            for h in range(NH):
                i = h % 2
                vector.wait_ge(sem("mm3"), h + 1)
                if h >= 2:
                    vector.wait_ge(sem("qw3"), 16 * (h - 1))
                base = 20 * h
                for sl in range(4):
                    L = sl * 512
                    R = (sl + 1) * 512
                    bb = base + 5 * sl
                    ser(vector, "rp3", bb + 0, lambda i=i, L=L, R=R: vector.tensor_tensor(
                        out=rtt[0:64, :], in0=ps_q[i][64:128, L:R], in1=sinT[:, L:R], op=ALU.mult))
                    ser(vector, "rp3", bb + 1, lambda i=i, L=L, R=R: vector.tensor_tensor(
                        out=rtt[64:128, :], in0=ps_q[i][0:64, L:R], in1=sinT[:, L:R], op=ALU.mult))
                    ser(vector, "rp3", bb + 2, lambda i=i, L=L, R=R: vector.tensor_tensor(
                        out=rkc[:], in0=ps_q[i][:, L:R], in1=cos2[:, L:R], op=ALU.mult))
                    ser(vector, "rp3", bb + 3, lambda i=i, L=L, R=R: vector.tensor_tensor(
                        out=qro[i][0:64, L:R], in0=rkc[0:64, :], in1=rtt[0:64, :], op=ALU.subtract))
                    ser(vector, "rp3", bb + 4, lambda i=i, L=L, R=R: vector.tensor_tensor(
                        out=qro[i][64:128, L:R], in0=rkc[64:128, :], in1=rtt[64:128, :], op=ALU.add))

        @block.sync
        def _(sync):
            for h in range(NH):
                sync.wait_ge(sem("rp3"), 20 * (h + 1))
                sync.dma_start(out=qT_s.ap()[h], in_=qro[h % 2][:]).then_inc(sem("qw3"), 16)

        es_p3.close()


    es_a.close()

    if upto == "p3":
        _emit_debug_tail(nc, sem, actT, out_t, dbg, qT_s, kT_s, vT_s, x2_s)
        return nc

    chunks = []
    dpref = 0
    for h in range(NH):
        for s in range(NSLAB):
            nch = 4 * s + 4
            for c in range(nch):
                diag = (c >= 4 * s)
                if diag:
                    dpref += 1
                chunks.append(dict(h=h, s=s, c=c, first=(c == 0), last=(c == nch - 1),
                                   diag=diag, idx=len(chunks), dpref=dpref,
                                   si=h * NSLAB + s))

    es_p4 = ExitStack()
    qTb = [es_p4.enter_context(nc.sbuf_tensor(f"qTb{i}", [128, T], BF16)) for i in range(3)]
    kTb = [es_p4.enter_context(nc.sbuf_tensor(f"kTb{i}", [128, T], BF16)) for i in range(2)]
    Vb = [es_p4.enter_context(nc.sbuf_tensor(f"Vb{i}", [128, NT, 128], BF16)) for i in range(2)]
    qsq = es_p4.enter_context(nc.sbuf_tensor("qsq", [128, T], BF16))
    ksq = es_p4.enter_context(nc.sbuf_tensor("ksq", [128, T], BF16))
    NPT = 6
    pTb = [es_p4.enter_context(nc.sbuf_tensor(f"pTb{i}", [128, SW], BF16)) for i in range(NPT)]
    rdn = [es_p4.enter_context(nc.sbuf_tensor(f"rdn{i}", [128, SW], F32)) for i in range(2)]
    lnd = [es_p4.enter_context(nc.sbuf_tensor(f"lnd{i}", [128, SW], F32)) for i in range(2)]
    mask_sb = es_p4.enter_context(nc.sbuf_tensor("mask_sb", [128, 4, SW], BF16))
    nbh = es_p4.enter_context(nc.sbuf_tensor("nbh", [128, NH], F32))
    nk_sb = es_p4.enter_context(nc.sbuf_tensor("nk_sb", [1, NKV], F32))
    sc4 = es_p4.enter_context(nc.sbuf_tensor("sc4", [1, 8], F32))
    ps_sT = [es_p4.enter_context(nc.psum_tensor(f"ps_sT{i}", [128, SW], F32)) for i in range(3)]
    ps_ctx = [es_p4.enter_context(nc.psum_tensor(f"ps_ctx{i}", [128, SW], F32)) for i in range(2)]
    ps_db = [es_p4.enter_context(nc.psum_tensor(f"ps_db{i}", [128, SW], F32)) for i in range(2)]
    ps_ms = es_p4.enter_context(nc.psum_tensor("ps_ms", [128, SW], F32))

    def prologue_ops(hp):
        ops = []
        kv = hp // 4
        if hp % 4 == 0:
            def f_ksq(e, kv=kv):
                e.wait_ge(sem("kld4"), 16 * (kv + 2))
                return e.activation(out=ksq[:], in_=kTb[kv % 2][:], func=AF.Square)
            ops.append(("scalar", f_ksq))
            for s4 in range(4):
                ops.append(("tensor", lambda e, s4=s4: e.matmul(
                    ps_ms[0:1, :], onesb[:, 0:1], ksq[:, s4 * 512:(s4 + 1) * 512],
                    start=True, stop=True)))
                if s4 == 0:
                    ops.append(("vector", lambda e: e.reduce_max(
                        out=sc4[0:1, 0:1], in_=ps_ms[0:1, :], axis=AX.X)))
                else:
                    ops.append(("vector", lambda e: e.reduce_max(
                        out=sc4[0:1, 7:8], in_=ps_ms[0:1, :], axis=AX.X)))
                    ops.append(("vector", lambda e: e.tensor_tensor(
                        out=sc4[0:1, 0:1], in0=sc4[0:1, 0:1], in1=sc4[0:1, 7:8], op=ALU.max)))
            ops.append(("vector", lambda e, kv=kv: e.tensor_copy(
                out=nk_sb[0:1, kv:kv + 1], in_=sc4[0:1, 0:1])))

        def f_qsq(e, hp=hp):
            e.wait_ge(sem("qld4"), 16 * (hp + 1))
            return e.activation(out=qsq[:], in_=qTb[hp % 3][:], func=AF.Square)
        ops.append(("scalar", f_qsq))
        for s4 in range(4):
            ops.append(("tensor", lambda e, s4=s4: e.matmul(
                ps_ms[0:1, :], onesb[:, 0:1], qsq[:, s4 * 512:(s4 + 1) * 512],
                start=True, stop=True)))
            if s4 == 0:
                ops.append(("vector", lambda e: e.reduce_max(
                    out=sc4[0:1, 0:1], in_=ps_ms[0:1, :], axis=AX.X)))
            else:
                ops.append(("vector", lambda e: e.reduce_max(
                    out=sc4[0:1, 7:8], in_=ps_ms[0:1, :], axis=AX.X)))
                ops.append(("vector", lambda e: e.tensor_tensor(
                    out=sc4[0:1, 0:1], in0=sc4[0:1, 0:1], in1=sc4[0:1, 7:8], op=ALU.max)))
        ops.append(("vector", lambda e, kv=kv: e.tensor_tensor(
            out=sc4[0:1, 1:2], in0=sc4[0:1, 0:1], in1=nk_sb[0:1, kv:kv + 1], op=ALU.mult)))
        ops.append(("vector", lambda e: e.tensor_scalar_mul(
            out=sc4[0:1, 1:2], in0=sc4[0:1, 1:2], scalar1=1.0 / 128.0)))
        ops.append(("scalar", lambda e: e.activation(
            out=sc4[0:1, 2:3], in_=sc4[0:1, 1:2], func=AF.Ln)))
        ops.append(("scalar", lambda e: e.activation(
            out=sc4[0:1, 3:4], in_=sc4[0:1, 2:3], func=AF.Exp, scale=0.5)))
        ops.append(("vector", lambda e: e.tensor_scalar(
            out=sc4[0:1, 4:5], in0=sc4[0:1, 3:4], scalar1=-1.0, scalar2=80.0,
            op0=ALU.mult, op1=ALU.add)))
        ops.append(("vector", lambda e: e.tensor_single_scalar(
            out=sc4[0:1, 5:6], in_=sc4[0:1, 4:5], scalar=0.0, op=ALU.min)))
        ops.append(("tensor", lambda e: e.matmul(
            ps_ms[0:128, 0:1], onesf[:], sc4[0:1, 5:6], start=True, stop=True)))
        ops.append(("scalar", lambda e, hp=hp: e.copy(
            out=nbh[:, hp:hp + 1], in_=ps_ms[0:128, 0:1])))
        return ops

    PR_LISTS = [prologue_ops(h) for h in range(NH)]
    PR_BASE = [0]
    for h in range(NH):
        PR_BASE.append(PR_BASE[-1] + len(PR_LISTS[h]))

    def emit_prologue(engname, eng, hp):
        base = PR_BASE[hp]
        for j, (en, fn) in enumerate(PR_LISTS[hp]):
            if en != engname:
                continue
            if base + j > 0:
                eng.wait_ge(sem("pr4"), base + j)
            fn(eng).then_inc(sem("pr4"), 1)

    with nc.Block("p4") as block:

        @block.sync
        def _(sync):
            sync.dma_start(out=mask_sb[:], in_=mask_c.ap().rearrange("(r p) q -> p r q", p=128)).then_inc(sem("kld4"), 16)
            sync.wait_ge(sem("wq3"), 32 * NH)
            sync.wait_ge(sem("kw2"), 16 * 1)
            sync.dma_start(out=kTb[0][:], in_=kT_s.ap()[0]).then_inc(sem("kld4"), 16)
            sync.wait_ge(sem("vw2"), 16 * 1)
            for c in range(NT):
                sync.dma_start(out=Vb[0][:, c, :], in_=vT_s.ap()[0][:, c * 128:(c + 1) * 128], transpose=True).then_inc(sem("vld4"), 16)
            for h in range(NH):
                if h >= 3:
                    sync.wait_ge(sem("hd4"), h - 2)
                sync.wait_ge(sem("qw3"), 16 * (h + 1))
                sync.dma_start(out=qTb[h % 3][:], in_=qT_s.ap()[h]).then_inc(sem("qld4"), 16)
                if h % 4 == 2 and h // 4 + 1 < NKV:
                    kvp = h // 4 + 1
                    sync.wait_ge(sem("hd4"), 4 * (kvp - 1))
                    sync.wait_ge(sem("kw2"), 16 * (kvp + 1))
                    sync.dma_start(out=kTb[kvp % 2][:], in_=kT_s.ap()[kvp]).then_inc(sem("kld4"), 16)
                    sync.wait_ge(sem("vw2"), 16 * (kvp + 1))
                    for c in range(NT):
                        sync.dma_start(out=Vb[kvp % 2][:, c, :], in_=vT_s.ap()[kvp][:, c * 128:(c + 1) * 128], transpose=True).then_inc(sem("vld4"), 16)

        @block.tensor
        def _(tensor):
            emit_prologue("tensor", tensor, 0)

            def emit_scores(ch):
                h, s, c, idx = ch["h"], ch["s"], ch["c"], ch["idx"]
                kv = h // 4
                if ch["first"] and s == 0:
                    tensor.wait_ge(sem("qld4"), 16 * (h + 1))
                    tensor.wait_ge(sem("kld4"), 16 * (kv + 2))
                if idx >= 3:
                    tensor.wait_ge(sem("exp4"), idx - 2)
                r0 = (c - 4 * s) * 128 if ch["diag"] else 0
                tensor.matmul(
                    ps_sT[idx % 3][:, r0:SW], kTb[kv % 2][:, c * 128:(c + 1) * 128],
                    qTb[h % 3][:, s * SW + r0:(s + 1) * SW], start=True, stop=True,
                ).then_inc(sem("sT4"), 1)

            for j in range(3):
                emit_scores(chunks[j])
            for ch in chunks:
                h, s, c, idx, si = ch["h"], ch["s"], ch["c"], ch["idx"], ch["si"]
                kv = h // 4
                if idx + 3 < len(chunks):
                    emit_scores(chunks[idx + 3])
                if ch["first"] and s == 2 and h + 1 < NH:
                    emit_prologue("tensor", tensor, h + 1)
                if ch["first"] and s == 0:
                    tensor.wait_ge(sem("vld4"), 16 * NT * (kv + 1))
                tensor.wait_ge(sem("exp4"), idx + 1)
                if ch["first"]:
                    if si >= 2:
                        tensor.wait_ge(sem("ln4"), si - 1)
                        tensor.wait_ge(sem("ce4"), si - 1)
                r0 = (c - 4 * s) * 128 if ch["diag"] else 0
                r1 = r0 + 128 if ch["diag"] else 0
                split = ch["diag"] and r1 < SW and not ch["first"]
                if split:
                    tensor.matmul(
                        ps_db[si % 2][:, r1:SW], onesq[:], pTb[idx % NPT][:, r1:SW],
                        start=False, stop=False,
                    )
                    tensor.matmul(
                        ps_ctx[si % 2][:, r1:SW], Vb[kv % 2][:, c, :], pTb[idx % NPT][:, r1:SW],
                        start=False, stop=False,
                    )
                if ch["diag"]:
                    tensor.wait_ge(sem("msk4"), ch["dpref"])
                    lo, hi = (r0, r1) if split else (r0, SW)
                else:
                    lo, hi = 0, SW
                tensor.matmul(
                    ps_db[si % 2][:, lo:hi], onesq[:], pTb[idx % NPT][:, lo:hi],
                    start=ch["first"], stop=ch["last"],
                )
                tensor.matmul(
                    ps_ctx[si % 2][:, lo:hi], Vb[kv % 2][:, c, :], pTb[idx % NPT][:, lo:hi],
                    start=ch["first"], stop=ch["last"],
                ).then_inc(sem("pec4"), 1)
                if ch["last"] and s == NSLAB - 1:
                    tensor.nop().then_inc(sem("hd4"), 1)

        @block.scalar
        def _(scalar):
            emit_prologue("scalar", scalar, 0)

            def slab_epilogue(si, idx_last):
                scalar.wait_ge(sem("pec4"), idx_last + 1)
                scalar.activation(
                    out=lnd[si % 2][:], in_=ps_db[si % 2][:], func=AF.Ln,
                ).then_inc(sem("ln4"), 1)
                scalar.wait_ge(sem("ln4"), si + 1)
                if si >= 2:
                    scalar.wait_ge(sem("ce4"), si - 1)
                scalar.activation(
                    out=rdn[si % 2][:], in_=lnd[si % 2][:], func=AF.Exp, scale=-1.0,
                ).then_inc(sem("rd4"), 1)

            pend = []
            for ch in chunks:
                h, s, c, idx, si = ch["h"], ch["s"], ch["c"], ch["idx"], ch["si"]
                if ch["first"] and s == 0:
                    scalar.wait_ge(sem("pr4"), PR_BASE[h + 1])
                scalar.wait_ge(sem("sT4"), idx + 1)
                if idx >= NPT - 1:
                    scalar.wait_ge(sem("pec4"), idx - (NPT - 1))
                r0 = (c - 4 * s) * 128 if ch["diag"] else 0
                scalar.activation(
                    out=pTb[idx % NPT][:, r0:SW], in_=ps_sT[idx % 3][:, r0:SW], func=AF.Exp,
                    scale=ISQ, bias=nbh[:, h:h + 1],
                ).then_inc(sem("exp4"), 1)
                if pend and idx - pend[0][1] >= 3:
                    psi, pidx = pend.pop(0)
                    slab_epilogue(psi, pidx)
                if ch["first"] and s == 2 and h + 1 < NH:
                    while pend:
                        psi, pidx = pend.pop(0)
                        slab_epilogue(psi, pidx)
                    emit_prologue("scalar", scalar, h + 1)
                if ch["last"]:
                    pend.append((si, idx))
            while pend:
                psi, pidx = pend.pop(0)
                slab_epilogue(psi, pidx)

        @block.vector
        def _(vector):
            emit_prologue("vector", vector, 0)
            for ch in chunks:
                h, s, c, idx, si = ch["h"], ch["s"], ch["c"], ch["idx"], ch["si"]
                if ch["first"] and s == 2 and h + 1 < NH:
                    emit_prologue("vector", vector, h + 1)
                if ch["diag"]:
                    r0 = (c - 4 * s) * 128
                    vector.wait_ge(sem("exp4"), idx + 1)
                    vector.tensor_tensor(
                        out=pTb[idx % NPT][:, r0:r0 + 128], in0=pTb[idx % NPT][:, r0:r0 + 128],
                        in1=mask_sb[:, 0, 0:128], op=ALU.mult,
                    ).then_inc(sem("msk4"), 1)
                if ch["last"]:
                    vector.wait_ge(sem("rd4"), si + 1)
                    vector.tensor_tensor(
                        out=actT[:, h * T + s * SW: h * T + (s + 1) * SW],
                        in0=ps_ctx[si % 2][:], in1=rdn[si % 2][:], op=ALU.mult,
                    ).then_inc(sem("ce4"), 1)

    es_p4.close()

    if upto == "p4":
        _emit_debug_tail(nc, sem, actT, out_t, dbg, qT_s, kT_s, vT_s, x2_s)
        return nc

    NOS5 = 8
    NG5 = 4
    EP5 = 6
    with nc.Block("p5") as block:
        es_p5 = ExitStack()
        wob = [es_p5.enter_context(nc.sbuf_tensor(f"wob{i}", [128, KC, 512], BF16)) for i in range(2)]
        xcol = [es_p5.enter_context(nc.sbuf_tensor(f"xcol{i}", [128, 512], F32)) for i in range(2)]
        x2col = [es_p5.enter_context(nc.sbuf_tensor(f"x2col{i}", [128, 512], F32)) for i in range(2)]
        sqcol = es_p5.enter_context(nc.sbuf_tensor("sqcol", [128, 512], F32))
        red5 = es_p5.enter_context(nc.sbuf_tensor("red5t", [128, 2], F32))
        ps5 = [es_p5.enter_context(nc.psum_tensor(f"ps5_{i}", [128, 512], F32)) for i in range(8)]

        @block.vector
        def _(vector):
            vector.memset(s1acc[:], 0.0).then_inc(sem("ep5"), 1)
            vector.memset(s2acc[:], 0.0).then_inc(sem("ep5"), 1)

        @block.gpsimd
        def _(gpsimd):
            for os in range(NOS5):
                if os >= 2:
                    gpsimd.wait_ge(sem("mm5"), (os - 1) * NG5)
                for qr in range(4):
                    src = Wo.ap().rearrange("(k p) m -> p k m", p=128)[:, qr * 8:(qr + 1) * 8, os * 512:(os + 1) * 512]
                    gpsimd.dma_start(out=wob[os % 2][:, qr * 8:(qr + 1) * 8, :], in_=src).then_inc(sem("wo5"), 16)

        @block.sync
        def _(sync):
            n = 0
            for os in range(NOS5):
                for g in range(NG5):
                    for tt in range(4):
                        t_g = g * 4 + tt
                        n += 1
                        if n >= 3:
                            sync.wait_ge(sem("ep5"), 2 + (n - 2) * EP5)
                        sync.dma_start(
                            out=xcol[n % 2][:],
                            in_=x_in.ap()[t_g * 128:(t_g + 1) * 128, os * 512:(os + 1) * 512],
                        ).then_inc(sem("xc5"), 16)

        @block.tensor
        def _(tensor):
            for os in range(NOS5):
                tensor.wait_ge(sem("wo5"), 64 * (os + 1))
                for g in range(NG5):
                    gi = os * NG5 + g
                    bank = (gi % 2) * 4
                    if gi >= 2:
                        tensor.wait_ge(sem("ep5"), 2 + ((gi - 2) * 4 + 4) * EP5)
                    for k in range(KC):
                        for tt in range(4):
                            t_g = g * 4 + tt
                            inst = tensor.matmul(
                                ps5[bank + tt][:],
                                actT[:, k * T + t_g * 128: k * T + (t_g + 1) * 128],
                                wob[os % 2][:, k, :],
                                start=(k == 0), stop=(k == KC - 1),
                            )
                            if k == KC - 1 and tt == 3:
                                inst.then_inc(sem("mm5"), 1)

        @block.vector
        def _(vector):
            n = 0
            for os in range(NOS5):
                for g in range(NG5):
                    gi = os * NG5 + g
                    vector.wait_ge(sem("mm5"), gi + 1)
                    bank = (gi % 2) * 4
                    for tt in range(4):
                        t_g = g * 4 + tt
                        n += 1
                        base = 2 + (n - 1) * EP5
                        vector.wait_ge(sem("xc5"), 16 * n)
                        if n >= 3:
                            vector.wait_ge(sem("x2w5"), 16 * (n - 2))
                        ser(vector, "ep5", base + 0, lambda bank=bank, tt=tt, n=n: vector.tensor_tensor(
                            out=x2col[n % 2][:], in0=ps5[bank + tt][:], in1=xcol[n % 2][:], op=ALU.add))
                        ser(vector, "ep5", base + 1, lambda n=n: vector.reduce_sum(
                            out=red5[:, 0:1], in_=x2col[n % 2][:], axis=AX.X))
                        ser(vector, "ep5", base + 3, lambda t_g=t_g: vector.tensor_tensor(
                            out=s1acc[:, t_g:t_g + 1], in0=s1acc[:, t_g:t_g + 1], in1=red5[:, 0:1], op=ALU.add))
                        ser(vector, "ep5", base + 4, lambda t_g=t_g: vector.tensor_tensor(
                            out=s2acc[:, t_g:t_g + 1], in0=s2acc[:, t_g:t_g + 1], in1=red5[:, 1:2], op=ALU.add))

        @block.scalar
        def _(scalar):
            n = 0
            for os in range(NOS5):
                for g in range(NG5):
                    for tt in range(4):
                        t_g = g * 4 + tt
                        n += 1
                        base = 2 + (n - 1) * EP5
                        ser(scalar, "ep5", base + 2, lambda n=n: scalar.activation(
                            out=sqcol[:], in_=x2col[n % 2][:], func=AF.Square,
                            accum_out=red5[:, 1:2]))
                        scalar.wait_ge(sem("ep5"), base + 5)
                        scalar.dma_start(
                            out=x2_s.ap()[t_g * 128:(t_g + 1) * 128, os * 512:(os + 1) * 512],
                            in_=x2col[n % 2][:],
                        ).then_inc(sem("x2w5"), 16)
                        scalar.nop().then_inc(sem("ep5"), 1)

        es_p5.close()

    if upto == "p5":
        _emit_debug_tail(nc, sem, actT, out_t, dbg, qT_s, kT_s, vT_s, x2_s)
        return nc

    es_c = ExitStack()
    AeffT = es_c.enter_context(nc.sbuf_tensor("AeffT", [128, KC * 16], BF16))
    BeffT = es_c.enter_context(nc.sbuf_tensor("BeffT", [16, KC * 128], BF16))
    tT_sb = es_c.enter_context(nc.sbuf_tensor("tT_sb", [16, T], BF16))

    with nc.Block("p6") as block:
        es_p6 = ExitStack()
        x2r = [es_p6.enter_context(nc.sbuf_tensor("x2r0", [128, D], F32))] * 2
        h2b = [es_p6.enter_context(nc.sbuf_tensor("h2b0", [128, D], BF16))] * 2
        Ast = es_p6.enter_context(nc.sbuf_tensor("Ast", [16, 3 * 1024], F32))
        aeff_c = es_p6.enter_context(nc.sbuf_tensor("aeff_c", [16, 1024], F32))
        aeffb = es_p6.enter_context(nc.sbuf_tensor("aeffb", [16, D], BF16))
        B_sb = es_p6.enter_context(nc.sbuf_tensor("B_sb", [128, 3, KC, 16], F32))
        beffb = es_p6.enter_context(nc.sbuf_tensor("beffb", [128, 16], BF16))
        mv_sb = es_p6.enter_context(nc.sbuf_tensor("mv_sb", [128, 3], F32))
        pt6 = [es_p6.enter_context(nc.psum_tensor(f"pt6_{i}", [128, 512], BF16)) for i in range(2)]
        ps6m = es_p6.enter_context(nc.psum_tensor("ps6m", [128, 512], F32))
        pt6b = es_p6.enter_context(nc.psum_tensor("pt6b", [128, 512], BF16))

        @block.vector
        def _(vector):
            ser(vector, "z6", 0, lambda: vector.tensor_scalar_mul(out=s1acc[:], in0=s1acc[:], scalar1=1.0 / D))
            ser(vector, "z6", 1, lambda: vector.tensor_scalar_mul(out=s2acc[:], in0=s2acc[:], scalar1=1.0 / D))
            ser(vector, "z6", 2, lambda: vector.tensor_tensor(out=rstd2[:], in0=s1acc[:], in1=s1acc[:], op=ALU.mult))
            ser(vector, "z6", 3, lambda: vector.tensor_tensor(out=rstd2[:], in0=rstd2[:], in1=s2acc[:], op=ALU.subtract))
            ser(vector, "z6", 4, lambda: vector.tensor_scalar(
                out=rstd2[:], in0=rstd2[:], scalar1=-1.0, scalar2=EPS, op0=ALU.mult, op1=ALU.add))
            ser(vector, "z6", 6, lambda: vector.reciprocal(out=rstd2[:], in_=negmr2[:]))
            ser(vector, "z6", 7, lambda: vector.scalar_tensor_tensor(
                out=negmr2[:], in0=s1acc[:], scalar=-1.0, in1=rstd2[:], op0=ALU.mult, op1=ALU.mult))

        @block.scalar
        def _(scalar):
            ser(scalar, "z6", 5, lambda: scalar.activation(
                out=negmr2[:], in_=rstd2[:], func=AF.Sqrt))

        @block.sync
        def _(sync):
            sync.wait_ge(sem("x2w5"), 16 * 128)
            for t in range(NT):
                if t >= 1:
                    sync.wait_ge(sem("ap6"), t)
                sync.dma_start(out=x2r[t % 2][:], in_=x2_s.ap()[t * 128:(t + 1) * 128, :]).then_inc(sem("xl6"), 16)

        @block.scalar
        def _(scalar):
            scalar.wait_ge(sem("z6"), 8)
            for t in range(NT):
                i = t % 2
                scalar.wait_ge(sem("xl6"), 16 * (t + 1))
                if t >= 1:
                    scalar.wait_ge(sem("tr6"), 8 * t)
                scalar.activation(
                    out=h2b[i][:], in_=x2r[i][:], func=AF.Identity,
                    scale=rstd2[:, t:t + 1], bias=negmr2[:, t:t + 1],
                ).then_inc(sem("ap6"), 1)
                for g in range(8):
                    gi = t * 8 + g
                    scalar.wait_ge(sem("tr6"), gi + 1)
                    dst = actT.ap().rearrange("p (c q) -> p c q", q=T)[:, 4 * g:4 * g + 4, t * 128:(t + 1) * 128]
                    scalar.copy(out=dst, in_=pt6[g % 2].ap().rearrange("p (c q) -> p c q", q=128)).then_inc(sem("ev6"), 1)

        @block.tensor
        def _(tensor):
            for t in range(NT):
                i = t % 2
                tensor.wait_ge(sem("ap6"), t + 1)
                for g in range(8):
                    gi = t * 8 + g
                    if gi >= 2:
                        tensor.wait_ge(sem("ev6"), gi - 1)
                    for j in range(4):
                        c = 4 * g + j
                        inst = tensor.transpose(
                            pt6[g % 2][:, j * 128:(j + 1) * 128],
                            h2b[i][:, c * 128:(c + 1) * 128],
                            idt[:],
                        )
                        if j == 3:
                            inst.then_inc(sem("tr6"), 1)

        lops = []
        cast_slot = {}

        def f_bcast(e):
            e.wait_ge(sem("z0"), 25)
            return e.matmul(ps6m[:, 0:3], onesf[:], mrow[:], start=True, stop=True)
        lops.append(("tensor", f_bcast))
        lops.append(("scalar", lambda e: e.copy(out=mv_sb[:], in_=ps6m[:, 0:3])))
        for cc in range(4):
            def f_a0(e, cc=cc):
                e.wait_ge(sem("g6a"), 48 * (cc + 1))
                return e.tensor_scalar(out=aeff_c[0:16, :], in0=Ast[0:16, 0:1024],
                                       scalar1=mv_sb[0:16, 0:1], scalar2=None, op0=ALU.mult)
            lops.append(("vector", f_a0))
            lops.append(("vector", lambda e: e.scalar_tensor_tensor(
                out=aeff_c[0:16, :], in0=Ast[0:16, 1024:2048], scalar=mv_sb[0:16, 1:2],
                in1=aeff_c[0:16, :], op0=ALU.mult, op1=ALU.add)))
            lops.append(("vector", lambda e: e.scalar_tensor_tensor(
                out=aeff_c[0:16, :], in0=Ast[0:16, 2048:3072], scalar=mv_sb[0:16, 2:3],
                in1=aeff_c[0:16, :], op0=ALU.mult, op1=ALU.add)))
            lops.append(("vector", lambda e, cc=cc: e.tensor_copy(
                out=aeffb[0:16, cc * 1024:(cc + 1) * 1024], in_=aeff_c[0:16, :])))
            cast_slot[cc] = len(lops) - 1
        for c in range(KC):
            lops.append(("tensor", lambda e, c=c: e.transpose(
                pt6b[:, c * 16:(c + 1) * 16], aeffb[0:16, c * 128:(c + 1) * 128], idt[0:16, 0:16])))
        lops.append(("scalar", lambda e: e.copy(out=AeffT[:], in_=pt6b[:, 0:512])))
        for c in range(KC):
            def f_b0(e, c=c):
                e.wait_ge(sem("g6"), 48)
                return e.tensor_scalar(out=beffb[:], in0=B_sb[:, 0, c, :],
                                       scalar1=mv_sb[:, 0:1], scalar2=None, op0=ALU.mult)
            lops.append(("vector", f_b0))
            lops.append(("vector", lambda e, c=c: e.scalar_tensor_tensor(
                out=beffb[:], in0=B_sb[:, 1, c, :], scalar=mv_sb[:, 1:2], in1=beffb[:],
                op0=ALU.mult, op1=ALU.add)))
            lops.append(("vector", lambda e, c=c: e.scalar_tensor_tensor(
                out=beffb[:], in0=B_sb[:, 2, c, :], scalar=mv_sb[:, 2:3], in1=beffb[:],
                op0=ALU.mult, op1=ALU.add)))
            lops.append(("tensor", lambda e, c=c: e.transpose(
                pt6b[0:16, (c % 4) * 128:((c % 4) + 1) * 128], beffb[:], idt[:])))
            if c % 4 == 3:
                lops.append(("scalar", lambda e, c=c: e.copy(
                    out=BeffT[0:16, (c - 3) * 128:(c + 1) * 128], in_=pt6b[0:16, 0:512])))
        run_oplist(block, "z6b", lops)

        @block.gpsimd
        def _(gpsimd):
            for a in range(3):
                gpsimd.dma_start(
                    out=B_sb[:, a, :, :],
                    in_=lB.ap()[a].rearrange("(c p) r -> p c r", p=128),
                ).then_inc(sem("g6"), 16)
            for cc in range(4):
                if cc >= 1:
                    gpsimd.wait_ge(sem("z6b"), cast_slot[cc - 1] + 1)
                for a in range(3):
                    gpsimd.dma_start(
                        out=Ast[0:16, a * 1024:(a + 1) * 1024],
                        in_=lA.ap()[a][:, cc * 1024:(cc + 1) * 1024],
                    ).then_inc(sem("g6a"), 16)

        es_p6.close()

    if upto == "p6":
        _emit_debug_tail(nc, sem, actT, out_t, dbg, qT_s, kT_s, vT_s, x2_s)
        return nc

    with nc.Block("p7") as block:
        es_p7 = ExitStack()
        wfth = [es_p7.enter_context(nc.sbuf_tensor(f"wfth{i}", [128, 16, 512], BF16)) for i in range(3)]
        bias_sb = es_p7.enter_context(nc.sbuf_tensor("bias_sb", [1, D], BF16))
        onesfb = es_p7.enter_context(nc.sbuf_tensor("onesfb", [1, 128], BF16))
        x2c7 = [es_p7.enter_context(nc.sbuf_tensor(f"x2c7_{i}", [128, 512], F32)) for i in range(2)]
        outc = [es_p7.enter_context(nc.sbuf_tensor(f"outc{i}", [128, 512], F32)) for i in range(2)]
        ps7 = [es_p7.enter_context(nc.psum_tensor(f"ps7_{i}", [128, 512], F32)) for i in range(8)]

        @block.scalar
        def _(scalar):
            for ts in range(4):
                scalar.wait_ge(sem("mm7"), ts + 1)
                scalar.copy(out=tT_sb[0:16, ts * 512:(ts + 1) * 512], in_=ps7[0][0:16, :]).then_inc(sem("tT7"), 1)

        @block.gpsimd
        def _(gpsimd):
            gpsimd.dma_start(out=bias_sb[:], in_=bias_t.ap()).then_inc(sem("l7"), 16)

        @block.scalar
        def _(scalar):
            scalar.dma_start(out=onesfb[:], in_=onesq_c.ap()[0:1, :]).then_inc(sem("l7"), 16)
            coords = []
            for os in range(8):
                for g in range(4):
                    for tt in range(4):
                        coords.append((g * 4 + tt, os))
            for n in (1, 2):
                t_g, os = coords[n - 1]
                scalar.dma_start(
                    out=x2c7[n % 2][:],
                    in_=x2_s.ap()[t_g * 128:(t_g + 1) * 128, os * 512:(os + 1) * 512],
                ).then_inc(sem("x2c7"), 16)
            for n_out in range(1, 129):
                if n_out + 2 <= 128:
                    n = n_out + 2
                    t_g, os = coords[n - 1]
                    scalar.wait_ge(sem("ev7"), n - 2)
                    scalar.dma_start(
                        out=x2c7[n % 2][:],
                        in_=x2_s.ap()[t_g * 128:(t_g + 1) * 128, os * 512:(os + 1) * 512],
                    ).then_inc(sem("x2c7"), 16)
                t_g, os = coords[n_out - 1]
                scalar.wait_ge(sem("ev7"), n_out)
                scalar.dma_start(
                    out=out_t.ap()[t_g * 128:(t_g + 1) * 128, os * 512:(os + 1) * 512],
                    in_=outc[n_out % 2][:],
                ).then_inc(sem("out7"), 16)

        @block.tensor
        def _(tensor):
            for ts in range(4):
                if ts >= 1:
                    tensor.wait_ge(sem("tT7"), ts)
                for k in range(KC):
                    inst = tensor.matmul(
                        ps7[0][0:16, :], AeffT[:, k * 16:(k + 1) * 16],
                        actT[:, k * T + ts * 512: k * T + (ts + 1) * 512],
                        start=(k == 0), stop=(k == KC - 1),
                    )
                    if k == KC - 1:
                        inst.then_inc(sem("mm7"), 1)
            for os in range(8):
                for g in range(4):
                    gi = os * 4 + g
                    bank = (gi % 2) * 4
                    if gi == 0:
                        tensor.wait_ge(sem("tT7"), 4)
                        tensor.wait_ge(sem("l7"), 32)
                    if gi >= 2:
                        tensor.wait_ge(sem("ev7"), (gi - 2) * 4 + 4)
                    for k in range(KC):
                        hh = os * 2 + k // 16
                        tensor.wait_ge(sem("wft7"), 16 * (os * KC + k + 1))
                        for tt in range(4):
                            t_g = g * 4 + tt
                            inst = tensor.matmul(
                                ps7[bank + tt][:],
                                actT[:, k * T + t_g * 128: k * T + (t_g + 1) * 128],
                                wfth[hh % 3][:, k % 16, :],
                                start=(k == 0), stop=False,
                            )
                            if g == 3 and tt == 3 and k % 16 == 15:
                                inst.then_inc(sem("osd7"), 1)
                    for tt in range(4):
                        t_g = g * 4 + tt
                        tensor.matmul(
                            ps7[bank + tt][:], onesfb[:], bias_sb[0:1, os * 512:(os + 1) * 512],
                            start=False, stop=False,
                        )
                        inst = tensor.matmul(
                            ps7[bank + tt][:], tT_sb[0:16, t_g * 128:(t_g + 1) * 128],
                            BeffT[0:16, os * 512:(os + 1) * 512],
                            start=False, stop=True,
                        )
                        if tt == 3:
                            inst.then_inc(sem("mm7"), 1)


        @block.sync
        def _(sync):
            sync.wait_ge(sem("wfw2"), 16 * 32)
            sync.wait_ge(sem("l7"), 32)
            for os in range(8):
                for half in range(2):
                    hh = os * 2 + half
                    if hh >= 3:
                        sync.wait_ge(sem("osd7"), hh - 2)
                    for kl in range(16):
                        k = half * 16 + kl
                        sync.dma_start(
                            out=wfth[hh % 3][:, kl, :],
                            in_=WfB.ap()[os * 512:(os + 1) * 512, k * 128:(k + 1) * 128],
                            transpose=True,
                        ).then_inc(sem("wft7"), 16)

        @block.vector
        def _(vector):
            n = 0
            for os in range(8):
                for g in range(4):
                    gi = os * 4 + g
                    bank = (gi % 2) * 4
                    vector.wait_ge(sem("mm7"), 4 + gi + 1)
                    for tt in range(4):
                        n += 1
                        vector.wait_ge(sem("x2c7"), 16 * n)
                        if n >= 3:
                            vector.wait_ge(sem("out7"), 16 * (n - 2))
                        vector.tensor_tensor(
                            out=outc[n % 2][:], in0=ps7[bank + tt][:], in1=x2c7[n % 2][:], op=ALU.add,
                        ).then_inc(sem("ev7"), 1)

        @block.gpsimd
        def _(gpsimd):
            gpsimd.wait_ge(sem("out7"), 16 * 128)

        es_p7.close()

    return nc


def _emit_debug_tail(nc, sem, actT, out_t, dbg, qT_s, kT_s, vT_s, x2_s):
    with nc.Block("dbgtail") as block:

        @block.gpsimd
        def _(gpsimd):
            dst = out_t.ap().rearrange("(a p) m -> p a m", p=128)
            gpsimd.dma_start(out=dst, in_=actT.ap().rearrange("p (a m) -> p a m", m=D)).then_inc(sem("dbg1"), 16)
            if dbg:
                for name, scr in [("qT", qT_s), ("kT", kT_s), ("vT", vT_s)]:
                    if name in dbg:
                        s = scr.ap()
                        d = dbg[name].ap()
                        gpsimd.dma_start(out=d, in_=s).then_inc(sem("dbg1"), 16)
                if "x2" in dbg:
                    gpsimd.dma_start(out=dbg["x2"].ap(), in_=x2_s.ap()).then_inc(sem("dbg1"), 16)
            n = 1 + (len(dbg) if dbg else 0)
            gpsimd.wait_ge(sem("dbg1"), 16 * n)


_NC_CACHE = {}


def _get_nc():
    if "nc" not in _NC_CACHE:
        _NC_CACHE["nc"] = build_nc()
    return _NC_CACHE["nc"]


def make_in_maps(inputs):
    x = np.ascontiguousarray(np.asarray(inputs["L0_attn_x"], dtype=np.float32))
    Wq = np.ascontiguousarray(np.asarray(inputs["L0_attn_Wq"], dtype=np.float32))
    Wk = np.ascontiguousarray(np.asarray(inputs["L0_attn_Wk"], dtype=np.float32))
    Wv = np.ascontiguousarray(np.asarray(inputs["L0_attn_Wv"], dtype=np.float32))
    Wo = np.ascontiguousarray(np.asarray(inputs["L0_attn_Wo"], dtype=np.float32))
    Wf = np.ascontiguousarray(np.asarray(inputs["L0_ffn_weight"], dtype=np.float32))
    fb = np.ascontiguousarray(np.asarray(inputs["L0_ffn_bias"], dtype=np.float32)).reshape(1, D)
    lAv = np.ascontiguousarray(np.asarray(inputs["L0_ffn_lora_A"], dtype=np.float32))
    lBv = np.ascontiguousarray(np.asarray(inputs["L0_ffn_lora_B"], dtype=np.float32))
    lsi = np.ascontiguousarray(np.asarray(inputs["L0_ffn_lora_start_indices"], dtype=np.int32)).reshape(1, 4)
    in_maps = []
    for b in range(B):
        bsel = np.zeros((1, 8), dtype=np.float32)
        bsel[0, b] = 1.0
        in_maps.append({
            "x": np.ascontiguousarray(x[b]), "Wq": Wq, "Wk": Wk, "Wv": Wv,
            "Wo": Wo, "Wf": Wf, "fbias": fb, "lA": lAv, "lB": lBv,
            "lsi": lsi, "bsel": bsel,
        })
    return in_maps


def kernel(**inputs):
    nc = _get_nc()
    in_maps = make_in_maps(inputs)
    res = run_bass_kernel_spmd(nc, in_maps, core_ids=list(range(B)))
    out = np.stack([res.results[b]["out"] for b in range(B)], axis=0)
    return out.astype(np.float32)


# version 17 (final)
# speedup vs baseline: 1.0509x; 1.0025x over previous
import numpy as np
import ml_dtypes
from contextlib import ExitStack

import concourse.bass as bass
import concourse.mybir as mybir
from concourse.bass_utils import run_bass_kernel_spmd

F32 = mybir.dt.float32
BF16 = mybir.dt.bfloat16
I32 = mybir.dt.int32
AF = mybir.ActivationFunctionType
ALU = mybir.AluOpType
AX = mybir.AxisListType
BF = ml_dtypes.bfloat16

B = 8
T = 2048
D = 4096
HD = 128
NH = 32
NKV = 8
KC = D // 128
NT = T // 128
SW = 512
NSLAB = T // SW
ISQ = float(1.0 / np.sqrt(128.0))
EPS = 1e-5
ROPE_BASE = 500000.0


def _host_tables():
    inv_freq = (1.0 / (ROPE_BASE ** (np.arange(0, HD, 2, dtype=np.float32) / HD))).astype(np.float32)
    t = np.arange(T, dtype=np.float32)
    freqs = np.einsum("i,j->ij", t, inv_freq).astype(np.float32)
    cos = np.cos(freqs).astype(np.float32)
    sin = np.sin(freqs).astype(np.float32)
    cosT = cos.T.copy()
    sinT = sin.T.copy()
    cos2 = np.concatenate([cosT, cosT], axis=0)
    masks = np.zeros((4, 128, SW), dtype=BF)
    for r in range(4):
        kl = np.arange(128)[:, None]
        ql = np.arange(SW)[None, :]
        masks[r] = ((r * 128 + kl) <= ql).astype(BF)
    return cos2, sinT, masks


def build_nc(upto="p7", debug=False):
    nc = bass.Bass("TRN2", target_bir_lowering=False)

    x_in = nc.dram_tensor("x", [T, D], F32, kind="ExternalInput")
    Wq = nc.dram_tensor("Wq", [D, D], F32, kind="ExternalInput")
    Wk = nc.dram_tensor("Wk", [D, NKV * HD], F32, kind="ExternalInput")
    Wv = nc.dram_tensor("Wv", [D, NKV * HD], F32, kind="ExternalInput")
    Wo = nc.dram_tensor("Wo", [D, D], F32, kind="ExternalInput")
    Wf = nc.dram_tensor("Wf", [D, D], F32, kind="ExternalInput")
    bias_t = nc.dram_tensor("fbias", [1, D], F32, kind="ExternalInput")
    lA = nc.dram_tensor("lA", [3, 16, D], F32, kind="ExternalInput")
    lB = nc.dram_tensor("lB", [3, D, 16], F32, kind="ExternalInput")
    lsi_t = nc.dram_tensor("lsi", [1, 4], I32, kind="ExternalInput")
    bsel_t = nc.dram_tensor("bsel", [1, 8], F32, kind="ExternalInput")
    out_t = nc.dram_tensor("out", [T, D], F32, kind="ExternalOutput")
    dbg = {}
    if debug:
        dbg["qT"] = nc.dram_tensor("d_qT", [NH, 128, T], F32, kind="ExternalOutput")
        dbg["kT"] = nc.dram_tensor("d_kT", [NKV, 128, T], F32, kind="ExternalOutput")
        dbg["vT"] = nc.dram_tensor("d_vT", [NKV, 128, T], F32, kind="ExternalOutput")
        dbg["x2"] = nc.dram_tensor("d_x2", [T, D], F32, kind="ExternalOutput")

    qT_s = nc.dram_tensor("qT_s", [NH, 128, T], BF16, kind="Internal")
    kT_s = nc.dram_tensor("kT_s", [NKV, 128, T], BF16, kind="Internal")
    vT_s = nc.dram_tensor("vT_s", [NKV, 128, T], BF16, kind="Internal")
    x2_s = nc.dram_tensor("x2_s", [T, D], F32, kind="Internal")
    WfB = nc.dram_tensor("WfB", [D, D], BF16, kind="Internal")

    cos2_np, sinT_np, masks_np = _host_tables()
    ident_c = nc.inline_tensor(np.eye(128).astype(BF), name="identc")
    iota8_c = nc.inline_tensor(np.arange(8, dtype=np.float32).reshape(1, 8), name="iota8c")
    onesb_c = nc.inline_tensor(np.ones((128, 1), dtype=BF), name="onesbc")
    onesq_c = nc.inline_tensor(np.ones((128, 128), dtype=BF), name="onesqc")
    onesf_c = nc.inline_tensor(np.ones((1, 128), dtype=np.float32), name="onesfc")
    cos2_c = nc.inline_tensor(cos2_np, name="cos2c")
    sinT_c = nc.inline_tensor(sinT_np, name="sinTc")
    mask_c = nc.inline_tensor(masks_np.reshape(4 * 128, SW), name="maskc")

    actT = nc.alloc_sbuf_tensor("actT", [128, KC * T], BF16)
    idt = nc.alloc_sbuf_tensor("idt", [128, 128], BF16)
    onesb = nc.alloc_sbuf_tensor("onesb", [128, 1], BF16)
    onesq = nc.alloc_sbuf_tensor("onesq", [128, 128], BF16)
    onesf = nc.alloc_sbuf_tensor("onesf", [1, 128], F32)
    iota8 = nc.alloc_sbuf_tensor("iota8", [1, 8], F32)
    bsel_sb = nc.alloc_sbuf_tensor("bsel_sb", [1, 8], F32)
    lsi_i = nc.alloc_sbuf_tensor("lsi_i", [1, 4], I32)
    lsi_f = nc.alloc_sbuf_tensor("lsi_f", [1, 4], F32)
    mrow = nc.alloc_sbuf_tensor("mrow", [1, 3], F32)
    rstd2 = nc.alloc_sbuf_tensor("rstd2", [128, NT], F32)
    negmr2 = nc.alloc_sbuf_tensor("negmr2", [128, NT], F32)
    s1acc = nc.alloc_sbuf_tensor("s1acc", [128, NT], F32)
    s2acc = nc.alloc_sbuf_tensor("s2acc", [128, NT], F32)

    SEM = {}

    def sem(name):
        if name not in SEM:
            SEM[name] = nc.alloc_semaphore(name)
        return SEM[name]

    def ser(eng, name, j, inst_fn):
        if j > 0:
            eng.wait_ge(sem(name), j)
        inst_fn().then_inc(sem(name), 1)
        return j + 1

    def run_oplist(block, name, oplist):
        for engname in ["sync", "gpsimd", "tensor", "vector", "scalar"]:
            ops = [(j, fn) for j, (en, fn) in enumerate(oplist) if en == engname]
            if not ops:
                continue

            def emit(e, ops=ops):
                for j, fn in ops:
                    if j > 0:
                        e.wait_ge(sem(name), j)
                    fn(e).then_inc(sem(name), 1)

            getattr(block, engname)(emit)

    es_a = ExitStack()
    cos2 = es_a.enter_context(nc.sbuf_tensor("cos2", [128, T], F32))
    sinT = es_a.enter_context(nc.sbuf_tensor("sinT", [64, T], F32))
    tmp8 = es_a.enter_context(nc.sbuf_tensor("tmp8", [1, 8 * 12], F32))
    rtt = es_a.enter_context(nc.sbuf_tensor("rtt", [128, 512], F32))
    rkc = es_a.enter_context(nc.sbuf_tensor("rkc", [128, 512], F32))

    def t8(i):
        return tmp8[0:1, i * 8:(i + 1) * 8]

    with nc.Block("p0") as block:

        @block.sync
        def _(sync):
            sync.dma_start(out=idt[:], in_=ident_c.ap()).then_inc(sem("l0"), 16)
            sync.dma_start(out=onesb[:], in_=onesb_c.ap()).then_inc(sem("l0"), 16)
            sync.dma_start(out=onesq[:], in_=onesq_c.ap()).then_inc(sem("l0"), 16)
            sync.dma_start(out=onesf[:], in_=onesf_c.ap()).then_inc(sem("l0"), 16)
            sync.dma_start(out=iota8[:], in_=iota8_c.ap()).then_inc(sem("l0"), 16)
            sync.dma_start(out=bsel_sb[:], in_=bsel_t.ap()).then_inc(sem("l0"), 16)
            sync.dma_start(out=lsi_i[:], in_=lsi_t.ap()).then_inc(sem("l0"), 16)
            sync.dma_start(out=cos2[:], in_=cos2_c.ap()).then_inc(sem("l0"), 16)
            sync.dma_start(out=sinT[:], in_=sinT_c.ap()).then_inc(sem("l0"), 16)

        @block.vector
        def _(vector):
            j = 0
            vector.wait_ge(sem("l0"), 16 * 7)
            j = ser(vector, "z0", j, lambda: vector.tensor_copy(out=lsi_f[:], in_=lsi_i[:]))
            for i in range(4):
                j = ser(vector, "z0", j, lambda i=i: vector.tensor_single_scalar(
                    out=t8(i), in_=iota8[:], scalar=lsi_f[0:1, i:i + 1], op=ALU.is_lt))
            j = ser(vector, "z0", j, lambda: vector.scalar_tensor_tensor(
                out=t8(4), in0=t8(0), scalar=-1.0, in1=t8(1), op0=ALU.mult, op1=ALU.mult))
            j = ser(vector, "z0", j, lambda: vector.tensor_tensor(out=t8(4), in0=t8(4), in1=t8(1), op=ALU.add))
            j = ser(vector, "z0", j, lambda: vector.tensor_scalar(
                out=t8(5), in0=t8(1), scalar1=-2.0, scalar2=2.0, op0=ALU.mult, op1=ALU.add))
            j = ser(vector, "z0", j, lambda: vector.tensor_tensor(out=t8(4), in0=t8(4), in1=t8(5), op=ALU.add))
            j = ser(vector, "z0", j, lambda: vector.tensor_scalar(
                out=t8(6), in0=t8(3), scalar1=-1.0, scalar2=4.0, op0=ALU.mult, op1=ALU.add))
            j = ser(vector, "z0", j, lambda: vector.tensor_tensor(out=t8(7), in0=t8(4), in1=t8(2), op=ALU.mult))
            j = ser(vector, "z0", j, lambda: vector.tensor_scalar(
                out=t8(8), in0=t8(2), scalar1=-1.0, scalar2=1.0, op0=ALU.mult, op1=ALU.add))
            j = ser(vector, "z0", j, lambda: vector.tensor_tensor(out=t8(8), in0=t8(8), in1=t8(6), op=ALU.mult))
            j = ser(vector, "z0", j, lambda: vector.tensor_tensor(out=t8(7), in0=t8(7), in1=t8(8), op=ALU.add))
            j = ser(vector, "z0", j, lambda: vector.tensor_single_scalar(
                out=t8(9), in_=t8(7), scalar=1.5, op=ALU.is_ge))
            j = ser(vector, "z0", j, lambda: vector.tensor_tensor(out=t8(9), in0=t8(9), in1=bsel_sb[:], op=ALU.mult))
            j = ser(vector, "z0", j, lambda: vector.tensor_scalar(
                out=t8(10), in0=t8(7), scalar1=-2.0, scalar2=0.0, op0=ALU.add, op1=ALU.max))
            j = ser(vector, "z0", j, lambda: vector.tensor_single_scalar(
                out=t8(10), in_=t8(10), scalar=2.0, op=ALU.min))
            for a in range(3):
                j = ser(vector, "z0", j, lambda a=a: vector.tensor_single_scalar(
                    out=t8(11), in_=t8(10), scalar=float(a), op=ALU.is_equal))
                j = ser(vector, "z0", j, lambda: vector.tensor_tensor(
                    out=t8(11), in0=t8(11), in1=t8(9), op=ALU.mult))
                j = ser(vector, "z0", j, lambda a=a: vector.reduce_sum(
                    out=mrow[0:1, a:a + 1], in_=t8(11), axis=AX.X))

    if upto == "p0":
        _emit_debug_tail(nc, sem, actT, out_t, dbg, qT_s, kT_s, vT_s, x2_s)
        return nc

    C1 = 10
    es_p1 = ExitStack()
    xt = [es_p1.enter_context(nc.sbuf_tensor(f"xt{i}", [128, D], F32)) for i in range(2)]
    xsq = [es_p1.enter_context(nc.sbuf_tensor("xsq0", [128, D], F32))] * 2
    h1b = [es_p1.enter_context(nc.sbuf_tensor("h1b0", [128, D], BF16))] * 2
    stt = es_p1.enter_context(nc.sbuf_tensor("sttiles", [128, 2, 8], F32))

    def stc(i, k):
        return stt[:, i, k:k + 1]

    with nc.Block("p1") as block:
        pt1 = [es_p1.enter_context(nc.psum_tensor(f"pt1_{i}", [128, 512], BF16)) for i in range(2)]

        @block.sync
        def _(sync):
            for t in range(NT):
                if t >= 2:
                    sync.wait_ge(sem("ap1"), t - 1)
                sync.dma_start(out=xt[t % 2][:], in_=x_in.ap()[t * 128:(t + 1) * 128, :]).then_inc(sem("xld1"), 16)

        @block.vector
        def _(vector):
            for t in range(NT):
                i = t % 2
                bt = 6 * t
                vector.wait_ge(sem("xld1"), 16 * (t + 1))
                if t >= 2:
                    vector.wait_ge(sem("ap1"), t - 1)
                vector.reduce_sum(out=stc(i, 0), in_=xt[i][:], axis=AX.X).then_inc(sem("st1"), 1)
                vector.wait_ge(sem("st1"), t + 1)
                if bt > 0:
                    vector.wait_ge(sem("sC1"), bt)
                vector.tensor_scalar_mul(out=stc(i, 2), in0=stc(i, 0), scalar1=1.0 / D).then_inc(sem("sC1"), 1)
                vector.wait_ge(sem("sA1"), t + 1)
                vector.wait_ge(sem("sC1"), bt + 1)
                vector.tensor_scalar_mul(out=stc(i, 3), in0=stc(i, 1), scalar1=1.0 / D).then_inc(sem("sC1"), 1)
                vector.wait_ge(sem("sC1"), bt + 2)
                vector.scalar_tensor_tensor(
                    out=stc(i, 4), in0=stc(i, 2), scalar=stc(i, 2), in1=stc(i, 3),
                    op0=ALU.mult, op1=ALU.subtract).then_inc(sem("sC1"), 1)
                vector.wait_ge(sem("sC1"), bt + 3)
                vector.tensor_scalar(
                    out=stc(i, 4), in0=stc(i, 4), scalar1=-1.0, scalar2=EPS,
                    op0=ALU.mult, op1=ALU.add).then_inc(sem("sC1"), 1)
                vector.wait_ge(sem("sD1"), t + 1)
                vector.reciprocal(out=stc(i, 6), in_=stc(i, 5)).then_inc(sem("sC1"), 1)
                vector.wait_ge(sem("sC1"), bt + 5)
                vector.scalar_tensor_tensor(
                    out=stc(i, 7), in0=stc(i, 2), scalar=-1.0, in1=stc(i, 6),
                    op0=ALU.mult, op1=ALU.mult).then_inc(sem("sC1"), 1)

        @block.scalar
        def _(scalar):
            for t in range(NT):
                i = t % 2
                bt = 6 * t
                scalar.wait_ge(sem("xld1"), 16 * (t + 1))
                if t >= 2:
                    scalar.wait_ge(sem("sC1"), 6 * (t - 2) + 2)
                scalar.activation(
                    out=xsq[i][:], in_=xt[i][:], func=AF.Square,
                    accum_out=stc(i, 1)).then_inc(sem("sA1"), 1)
                scalar.wait_ge(sem("sC1"), bt + 4)
                scalar.activation(
                    out=stc(i, 5), in_=stc(i, 4), func=AF.Sqrt).then_inc(sem("sD1"), 1)
                scalar.wait_ge(sem("sC1"), bt + 6)
                if t >= 1:
                    scalar.wait_ge(sem("tr1"), 8 * t)
                scalar.activation(
                    out=h1b[i][:], in_=xt[i][:], func=AF.Identity,
                    scale=stc(i, 6), bias=stc(i, 7),
                ).then_inc(sem("ap1"), 1)
                for g in range(8):
                    gi = t * 8 + g
                    scalar.wait_ge(sem("tr1"), gi + 1)
                    dst = actT.ap().rearrange("p (c q) -> p c q", q=T)[:, 4 * g:4 * g + 4, t * 128:(t + 1) * 128]
                    scalar.copy(out=dst, in_=pt1[g % 2].ap().rearrange("p (c q) -> p c q", q=128)).then_inc(sem("ev1"), 1)

        @block.tensor
        def _(tensor):
            for t in range(NT):
                i = t % 2
                tensor.wait_ge(sem("ap1"), t + 1)
                for g in range(8):
                    gi = t * 8 + g
                    if gi >= 2:
                        tensor.wait_ge(sem("ev1"), gi - 1)
                    for j in range(4):
                        c = 4 * g + j
                        inst = tensor.transpose(
                            pt1[g % 2][:, j * 128:(j + 1) * 128],
                            h1b[i][:, c * 128:(c + 1) * 128],
                            idt[:],
                        )
                        if j == 3:
                            inst.then_inc(sem("tr1"), 1)

        es_p1.close()

    if upto == "p1":
        _emit_debug_tail(nc, sem, actT, out_t, dbg, qT_s, kT_s, vT_s, x2_s)
        return nc

    UNITS2 = []
    for m in range(NKV):
        UNITS2.append(("k", m))
        UNITS2.append(("v", m))
    with nc.Block("p2") as block:
        es_p2 = ExitStack()
        wub = [es_p2.enter_context(nc.sbuf_tensor(f"wub{i}", [128, KC, 128], BF16)) for i in range(2)]
        kro = es_p2.enter_context(nc.sbuf_tensor("kro0", [128, T], BF16))
        vbs = es_p2.enter_context(nc.sbuf_tensor("vbs0", [128, T], BF16))
        wfc = es_p2.enter_context(nc.sbuf_tensor("wfc0", [128, D], BF16))
        ps_u = [es_p2.enter_context(nc.psum_tensor(f"ps_u{i}", [128, T], F32)) for i in range(2)]

        @block.gpsimd
        def _(gpsimd):
            for u, (typ, m) in enumerate(UNITS2):
                if u >= 2:
                    gpsimd.wait_ge(sem("mm2"), u - 1)
                W_src = {"k": Wk, "v": Wv}[typ]
                for half in range(2):
                    srcw = W_src.ap().rearrange("(k p) m -> p k m", p=128)[:, half * 16:(half + 1) * 16, m * 128:(m + 1) * 128]
                    gpsimd.dma_start(out=wub[u % 2][:, half * 16:(half + 1) * 16, :], in_=srcw).then_inc(sem("wk2"), 16)
                if typ == "v":
                    for j in range(2):
                        cc = m * 2 + j
                        for sub in range(2):
                            c = cc * 2 + sub
                            if c >= 1:
                                gpsimd.wait_ge(sem("wfw2"), 16 * c)
                            gpsimd.dma_start(out=wfc[:], in_=Wf.ap()[c * 128:(c + 1) * 128, :]).then_inc(sem("wfc2"), 16)

        @block.scalar
        def _(scalar):
            nv = 0
            for u, (typ, m) in enumerate(UNITS2):
                if typ == "v":
                    for j in range(4):
                        c = m * 4 + j
                        scalar.wait_ge(sem("wfc2"), 16 * (c + 1))
                        scalar.dma_start(out=WfB.ap()[c * 128:(c + 1) * 128, :], in_=wfc[:]).then_inc(sem("wfw2"), 16)
                    nv += 1
                    scalar.wait_ge(sem("mm2"), u + 1)
                    if nv >= 2:
                        scalar.wait_ge(sem("vw2"), 16 * (nv - 1))
                    scalar.activation(out=vbs[:], in_=ps_u[u % 2][:], func=AF.Copy).then_inc(sem("vc2"), 1)

        @block.tensor
        def _(tensor):
            for u, (typ, m) in enumerate(UNITS2):
                tensor.wait_ge(sem("wk2"), 32 * (u + 1))
                if u >= 2:
                    prev = u - 2
                    if prev % 2 == 0:
                        tensor.wait_ge(sem("rp2"), 20 * (prev // 2) + 18)
                    else:
                        tensor.wait_ge(sem("vc2"), (prev + 1) // 2)
                for k in range(KC):
                    for s4 in range(4):
                        inst = tensor.matmul(
                            ps_u[u % 2][:, s4 * 512:(s4 + 1) * 512], wub[u % 2][:, k, :],
                            actT[:, k * T + s4 * 512: k * T + (s4 + 1) * 512],
                            start=(k == 0), stop=(k == KC - 1),
                        )
                        if k == KC - 1 and s4 == 3:
                            inst.then_inc(sem("mm2"), 1)

        @block.vector
        def _(vector):
            nk = 0
            for u, (typ, m) in enumerate(UNITS2):
                if typ == "v":
                    continue
                nk += 1
                vector.wait_ge(sem("mm2"), u + 1)
                if nk >= 2:
                    vector.wait_ge(sem("kw2"), 16 * (nk - 1))
                base = 20 * (nk - 1)
                for sl in range(4):
                    L = sl * 512
                    R = (sl + 1) * 512
                    bb = base + 5 * sl
                    ser(vector, "rp2", bb + 0, lambda u=u, L=L, R=R: vector.tensor_tensor(
                        out=rtt[0:64, :], in0=ps_u[u % 2][64:128, L:R], in1=sinT[:, L:R], op=ALU.mult))
                    ser(vector, "rp2", bb + 1, lambda u=u, L=L, R=R: vector.tensor_tensor(
                        out=rtt[64:128, :], in0=ps_u[u % 2][0:64, L:R], in1=sinT[:, L:R], op=ALU.mult))
                    ser(vector, "rp2", bb + 2, lambda u=u, L=L, R=R: vector.tensor_tensor(
                        out=rkc[:], in0=ps_u[u % 2][:, L:R], in1=cos2[:, L:R], op=ALU.mult))
                    ser(vector, "rp2", bb + 3, lambda L=L, R=R: vector.tensor_tensor(
                        out=kro[0:64, L:R], in0=rkc[0:64, :], in1=rtt[0:64, :], op=ALU.subtract))
                    ser(vector, "rp2", bb + 4, lambda L=L, R=R: vector.tensor_tensor(
                        out=kro[64:128, L:R], in0=rkc[64:128, :], in1=rtt[64:128, :], op=ALU.add))

        @block.sync
        def _(sync):
            nk = 0
            nv = 0
            for u, (typ, m) in enumerate(UNITS2):
                if typ == "k":
                    nk += 1
                    sync.wait_ge(sem("rp2"), 20 * nk)
                    sync.dma_start(out=kT_s.ap()[m], in_=kro[:]).then_inc(sem("kw2"), 16)
                else:
                    nv += 1
                    sync.wait_ge(sem("vc2"), nv)
                    sync.dma_start(out=vT_s.ap()[m], in_=vbs[:]).then_inc(sem("vw2"), 16)

        es_p2.close()

    if upto == "p2":
        _emit_debug_tail(nc, sem, actT, out_t, dbg, qT_s, kT_s, vT_s, x2_s)
        return nc

    with nc.Block("p3") as block:
        es_p3 = ExitStack()
        wqb = [es_p3.enter_context(nc.sbuf_tensor(f"wqb{i}", [128, KC, 128], BF16)) for i in range(2)]
        qro = [es_p3.enter_context(nc.sbuf_tensor(f"qro{i}", [128, T], BF16)) for i in range(2)]
        ps_q = [es_p3.enter_context(nc.psum_tensor(f"ps_q{i}", [128, T], F32)) for i in range(2)]

        @block.gpsimd
        def _(gpsimd):
            for h in range(NH):
                if h >= 2:
                    gpsimd.wait_ge(sem("mm3"), h - 1)
                for half in range(2):
                    src = Wq.ap().rearrange("(k p) m -> p k m", p=128)[:, half * 16:(half + 1) * 16, h * 128:(h + 1) * 128]
                    gpsimd.dma_start(out=wqb[h % 2][:, half * 16:(half + 1) * 16, :], in_=src).then_inc(sem("wq3"), 16)

        @block.tensor
        def _(tensor):
            for h in range(NH):
                tensor.wait_ge(sem("wq3"), 32 * (h + 1))
                if h >= 2:
                    tensor.wait_ge(sem("rp3"), 20 * (h - 2) + 18)
                for k in range(KC):
                    for s in range(4):
                        inst = tensor.matmul(
                            ps_q[h % 2][:, s * 512:(s + 1) * 512], wqb[h % 2][:, k, :],
                            actT[:, k * T + s * 512: k * T + (s + 1) * 512],
                            start=(k == 0), stop=(k == KC - 1),
                        )
                        if k == KC - 1 and s == 3:
                            inst.then_inc(sem("mm3"), 1)

        @block.vector
        def _(vector):
            for h in range(NH):
                i = h % 2
                vector.wait_ge(sem("mm3"), h + 1)
                if h >= 2:
                    vector.wait_ge(sem("qw3"), 16 * (h - 1))
                base = 20 * h
                for sl in range(4):
                    L = sl * 512
                    R = (sl + 1) * 512
                    bb = base + 5 * sl
                    ser(vector, "rp3", bb + 0, lambda i=i, L=L, R=R: vector.tensor_tensor(
                        out=rtt[0:64, :], in0=ps_q[i][64:128, L:R], in1=sinT[:, L:R], op=ALU.mult))
                    ser(vector, "rp3", bb + 1, lambda i=i, L=L, R=R: vector.tensor_tensor(
                        out=rtt[64:128, :], in0=ps_q[i][0:64, L:R], in1=sinT[:, L:R], op=ALU.mult))
                    ser(vector, "rp3", bb + 2, lambda i=i, L=L, R=R: vector.tensor_tensor(
                        out=rkc[:], in0=ps_q[i][:, L:R], in1=cos2[:, L:R], op=ALU.mult))
                    ser(vector, "rp3", bb + 3, lambda i=i, L=L, R=R: vector.tensor_tensor(
                        out=qro[i][0:64, L:R], in0=rkc[0:64, :], in1=rtt[0:64, :], op=ALU.subtract))
                    ser(vector, "rp3", bb + 4, lambda i=i, L=L, R=R: vector.tensor_tensor(
                        out=qro[i][64:128, L:R], in0=rkc[64:128, :], in1=rtt[64:128, :], op=ALU.add))

        @block.sync
        def _(sync):
            for h in range(NH):
                sync.wait_ge(sem("rp3"), 20 * (h + 1))
                sync.dma_start(out=qT_s.ap()[h], in_=qro[h % 2][:]).then_inc(sem("qw3"), 16)

        es_p3.close()


    es_a.close()

    if upto == "p3":
        _emit_debug_tail(nc, sem, actT, out_t, dbg, qT_s, kT_s, vT_s, x2_s)
        return nc

    chunks = []
    dpref = 0
    for h in range(NH):
        for s in range(NSLAB):
            nch = 4 * s + 4
            for c in range(nch):
                diag = (c >= 4 * s)
                if diag:
                    dpref += 1
                chunks.append(dict(h=h, s=s, c=c, first=(c == 0), last=(c == nch - 1),
                                   diag=diag, idx=len(chunks), dpref=dpref,
                                   si=h * NSLAB + s))

    es_p4 = ExitStack()
    qTb = [es_p4.enter_context(nc.sbuf_tensor(f"qTb{i}", [128, T], BF16)) for i in range(3)]
    kTb = [es_p4.enter_context(nc.sbuf_tensor(f"kTb{i}", [128, T], BF16)) for i in range(2)]
    Vb = [es_p4.enter_context(nc.sbuf_tensor(f"Vb{i}", [128, NT, 128], BF16)) for i in range(2)]
    qsq = es_p4.enter_context(nc.sbuf_tensor("qsq", [128, T], BF16))
    ksq = es_p4.enter_context(nc.sbuf_tensor("ksq", [128, T], BF16))
    NPT = 6
    pTb = [es_p4.enter_context(nc.sbuf_tensor(f"pTb{i}", [128, SW], BF16)) for i in range(NPT)]
    rdn = [es_p4.enter_context(nc.sbuf_tensor(f"rdn{i}", [128, SW], F32)) for i in range(2)]
    lnd = [es_p4.enter_context(nc.sbuf_tensor(f"lnd{i}", [128, SW], F32)) for i in range(2)]
    mask_sb = es_p4.enter_context(nc.sbuf_tensor("mask_sb", [128, 4, SW], BF16))
    nbh = es_p4.enter_context(nc.sbuf_tensor("nbh", [128, NH], F32))
    nk_sb = es_p4.enter_context(nc.sbuf_tensor("nk_sb", [1, NKV], F32))
    sc4 = es_p4.enter_context(nc.sbuf_tensor("sc4", [1, 8], F32))
    ps_sT = [es_p4.enter_context(nc.psum_tensor(f"ps_sT{i}", [128, SW], F32)) for i in range(3)]
    ps_ctx = [es_p4.enter_context(nc.psum_tensor(f"ps_ctx{i}", [128, SW], F32)) for i in range(2)]
    ps_db = [es_p4.enter_context(nc.psum_tensor(f"ps_db{i}", [128, SW], F32)) for i in range(2)]
    ps_ms = es_p4.enter_context(nc.psum_tensor("ps_ms", [128, SW], F32))

    def prologue_ops(hp):
        ops = []
        kv = hp // 4
        if hp % 4 == 0:
            def f_ksq(e, kv=kv):
                e.wait_ge(sem("kld4"), 16 * (kv + 2))
                return e.activation(out=ksq[:], in_=kTb[kv % 2][:], func=AF.Square)
            ops.append(("scalar", f_ksq))
            for s4 in range(4):
                ops.append(("tensor", lambda e, s4=s4: e.matmul(
                    ps_ms[0:1, :], onesb[:, 0:1], ksq[:, s4 * 512:(s4 + 1) * 512],
                    start=True, stop=True)))
                if s4 == 0:
                    ops.append(("vector", lambda e: e.reduce_max(
                        out=sc4[0:1, 0:1], in_=ps_ms[0:1, :], axis=AX.X)))
                else:
                    ops.append(("vector", lambda e: e.reduce_max(
                        out=sc4[0:1, 7:8], in_=ps_ms[0:1, :], axis=AX.X)))
                    ops.append(("vector", lambda e: e.tensor_tensor(
                        out=sc4[0:1, 0:1], in0=sc4[0:1, 0:1], in1=sc4[0:1, 7:8], op=ALU.max)))
            ops.append(("vector", lambda e, kv=kv: e.tensor_copy(
                out=nk_sb[0:1, kv:kv + 1], in_=sc4[0:1, 0:1])))

        def f_qsq(e, hp=hp):
            e.wait_ge(sem("qld4"), 16 * (hp + 1))
            return e.activation(out=qsq[:], in_=qTb[hp % 3][:], func=AF.Square)
        ops.append(("scalar", f_qsq))
        for s4 in range(4):
            ops.append(("tensor", lambda e, s4=s4: e.matmul(
                ps_ms[0:1, :], onesb[:, 0:1], qsq[:, s4 * 512:(s4 + 1) * 512],
                start=True, stop=True)))
            if s4 == 0:
                ops.append(("vector", lambda e: e.reduce_max(
                    out=sc4[0:1, 0:1], in_=ps_ms[0:1, :], axis=AX.X)))
            else:
                ops.append(("vector", lambda e: e.reduce_max(
                    out=sc4[0:1, 7:8], in_=ps_ms[0:1, :], axis=AX.X)))
                ops.append(("vector", lambda e: e.tensor_tensor(
                    out=sc4[0:1, 0:1], in0=sc4[0:1, 0:1], in1=sc4[0:1, 7:8], op=ALU.max)))
        ops.append(("vector", lambda e, kv=kv: e.tensor_tensor(
            out=sc4[0:1, 1:2], in0=sc4[0:1, 0:1], in1=nk_sb[0:1, kv:kv + 1], op=ALU.mult)))
        ops.append(("vector", lambda e: e.tensor_scalar_mul(
            out=sc4[0:1, 1:2], in0=sc4[0:1, 1:2], scalar1=1.0 / 128.0)))
        ops.append(("scalar", lambda e: e.activation(
            out=sc4[0:1, 2:3], in_=sc4[0:1, 1:2], func=AF.Ln)))
        ops.append(("scalar", lambda e: e.activation(
            out=sc4[0:1, 3:4], in_=sc4[0:1, 2:3], func=AF.Exp, scale=0.5)))
        ops.append(("vector", lambda e: e.tensor_scalar(
            out=sc4[0:1, 4:5], in0=sc4[0:1, 3:4], scalar1=-1.0, scalar2=80.0,
            op0=ALU.mult, op1=ALU.add)))
        ops.append(("vector", lambda e: e.tensor_single_scalar(
            out=sc4[0:1, 5:6], in_=sc4[0:1, 4:5], scalar=0.0, op=ALU.min)))
        ops.append(("tensor", lambda e: e.matmul(
            ps_ms[0:128, 0:1], onesf[:], sc4[0:1, 5:6], start=True, stop=True)))
        ops.append(("scalar", lambda e, hp=hp: e.copy(
            out=nbh[:, hp:hp + 1], in_=ps_ms[0:128, 0:1])))
        return ops

    PR_LISTS = [prologue_ops(h) for h in range(NH)]
    PR_BASE = [0]
    for h in range(NH):
        PR_BASE.append(PR_BASE[-1] + len(PR_LISTS[h]))

    def emit_prologue(engname, eng, hp):
        base = PR_BASE[hp]
        for j, (en, fn) in enumerate(PR_LISTS[hp]):
            if en != engname:
                continue
            if base + j > 0:
                eng.wait_ge(sem("pr4"), base + j)
            fn(eng).then_inc(sem("pr4"), 1)

    with nc.Block("p4") as block:

        @block.sync
        def _(sync):
            sync.dma_start(out=mask_sb[:], in_=mask_c.ap().rearrange("(r p) q -> p r q", p=128)).then_inc(sem("kld4"), 16)
            sync.wait_ge(sem("wq3"), 32 * NH)
            sync.wait_ge(sem("kw2"), 16 * 1)
            sync.dma_start(out=kTb[0][:], in_=kT_s.ap()[0]).then_inc(sem("kld4"), 16)
            sync.wait_ge(sem("vw2"), 16 * 1)
            for c in range(NT):
                sync.dma_start(out=Vb[0][:, c, :], in_=vT_s.ap()[0][:, c * 128:(c + 1) * 128], transpose=True).then_inc(sem("vld4"), 16)
            for h in range(NH):
                if h >= 3:
                    sync.wait_ge(sem("hd4"), h - 2)
                sync.wait_ge(sem("qw3"), 16 * (h + 1))
                sync.dma_start(out=qTb[h % 3][:], in_=qT_s.ap()[h]).then_inc(sem("qld4"), 16)
                if h % 4 == 2 and h // 4 + 1 < NKV:
                    kvp = h // 4 + 1
                    sync.wait_ge(sem("hd4"), 4 * (kvp - 1))
                    sync.wait_ge(sem("kw2"), 16 * (kvp + 1))
                    sync.dma_start(out=kTb[kvp % 2][:], in_=kT_s.ap()[kvp]).then_inc(sem("kld4"), 16)
                    sync.wait_ge(sem("vw2"), 16 * (kvp + 1))
                    for c in range(NT):
                        sync.dma_start(out=Vb[kvp % 2][:, c, :], in_=vT_s.ap()[kvp][:, c * 128:(c + 1) * 128], transpose=True).then_inc(sem("vld4"), 16)

        @block.tensor
        def _(tensor):
            emit_prologue("tensor", tensor, 0)

            def emit_scores(ch):
                h, s, c, idx = ch["h"], ch["s"], ch["c"], ch["idx"]
                kv = h // 4
                if ch["first"] and s == 0:
                    tensor.wait_ge(sem("qld4"), 16 * (h + 1))
                    tensor.wait_ge(sem("kld4"), 16 * (kv + 2))
                if idx >= 3:
                    tensor.wait_ge(sem("exp4"), idx - 2)
                r0 = (c - 4 * s) * 128 if ch["diag"] else 0
                tensor.matmul(
                    ps_sT[idx % 3][:, r0:SW], kTb[kv % 2][:, c * 128:(c + 1) * 128],
                    qTb[h % 3][:, s * SW + r0:(s + 1) * SW], start=True, stop=True,
                ).then_inc(sem("sT4"), 1)

            for j in range(3):
                emit_scores(chunks[j])
            for ch in chunks:
                h, s, c, idx, si = ch["h"], ch["s"], ch["c"], ch["idx"], ch["si"]
                kv = h // 4
                if idx + 3 < len(chunks):
                    emit_scores(chunks[idx + 3])
                if ch["first"] and s == 2 and h + 1 < NH:
                    emit_prologue("tensor", tensor, h + 1)
                if ch["first"] and s == 0:
                    tensor.wait_ge(sem("vld4"), 16 * NT * (kv + 1))
                tensor.wait_ge(sem("exp4"), idx + 1)
                if ch["first"]:
                    if si >= 2:
                        tensor.wait_ge(sem("ln4"), si - 1)
                        tensor.wait_ge(sem("ce4"), si - 1)
                r0 = (c - 4 * s) * 128 if ch["diag"] else 0
                r1 = r0 + 128 if ch["diag"] else 0
                split = ch["diag"] and r1 < SW and not ch["first"]
                if split:
                    tensor.matmul(
                        ps_db[si % 2][:, r1:SW], onesq[:], pTb[idx % NPT][:, r1:SW],
                        start=False, stop=False,
                    )
                    tensor.matmul(
                        ps_ctx[si % 2][:, r1:SW], Vb[kv % 2][:, c, :], pTb[idx % NPT][:, r1:SW],
                        start=False, stop=False,
                    )
                if ch["diag"]:
                    tensor.wait_ge(sem("msk4"), ch["dpref"])
                    lo, hi = (r0, r1) if split else (r0, SW)
                else:
                    lo, hi = 0, SW
                tensor.matmul(
                    ps_db[si % 2][:, lo:hi], onesq[:], pTb[idx % NPT][:, lo:hi],
                    start=ch["first"], stop=ch["last"],
                )
                tensor.matmul(
                    ps_ctx[si % 2][:, lo:hi], Vb[kv % 2][:, c, :], pTb[idx % NPT][:, lo:hi],
                    start=ch["first"], stop=ch["last"],
                ).then_inc(sem("pec4"), 1)
                if ch["last"] and s == NSLAB - 1:
                    tensor.nop().then_inc(sem("hd4"), 1)

        @block.scalar
        def _(scalar):
            emit_prologue("scalar", scalar, 0)

            def slab_epilogue(si, idx_last):
                scalar.wait_ge(sem("pec4"), idx_last + 1)
                scalar.activation(
                    out=lnd[si % 2][:], in_=ps_db[si % 2][:], func=AF.Ln,
                ).then_inc(sem("ln4"), 1)
                scalar.wait_ge(sem("ln4"), si + 1)
                if si >= 2:
                    scalar.wait_ge(sem("ce4"), si - 1)
                scalar.activation(
                    out=rdn[si % 2][:], in_=lnd[si % 2][:], func=AF.Exp, scale=-1.0,
                ).then_inc(sem("rd4"), 1)

            pend = []
            for ch in chunks:
                h, s, c, idx, si = ch["h"], ch["s"], ch["c"], ch["idx"], ch["si"]
                if ch["first"] and s == 0:
                    scalar.wait_ge(sem("pr4"), PR_BASE[h + 1])
                scalar.wait_ge(sem("sT4"), idx + 1)
                if idx >= NPT - 1:
                    scalar.wait_ge(sem("pec4"), idx - (NPT - 1))
                r0 = (c - 4 * s) * 128 if ch["diag"] else 0
                scalar.activation(
                    out=pTb[idx % NPT][:, r0:SW], in_=ps_sT[idx % 3][:, r0:SW], func=AF.Exp,
                    scale=ISQ, bias=nbh[:, h:h + 1],
                ).then_inc(sem("exp4"), 1)
                if pend and idx - pend[0][1] >= 3:
                    psi, pidx = pend.pop(0)
                    slab_epilogue(psi, pidx)
                if ch["first"] and s == 2 and h + 1 < NH:
                    while pend:
                        psi, pidx = pend.pop(0)
                        slab_epilogue(psi, pidx)
                    emit_prologue("scalar", scalar, h + 1)
                if ch["last"]:
                    pend.append((si, idx))
            while pend:
                psi, pidx = pend.pop(0)
                slab_epilogue(psi, pidx)

        @block.vector
        def _(vector):
            emit_prologue("vector", vector, 0)
            for ch in chunks:
                h, s, c, idx, si = ch["h"], ch["s"], ch["c"], ch["idx"], ch["si"]
                if ch["first"] and s == 2 and h + 1 < NH:
                    emit_prologue("vector", vector, h + 1)
                if ch["diag"]:
                    r0 = (c - 4 * s) * 128
                    vector.wait_ge(sem("exp4"), idx + 1)
                    vector.tensor_tensor(
                        out=pTb[idx % NPT][:, r0:r0 + 128], in0=pTb[idx % NPT][:, r0:r0 + 128],
                        in1=mask_sb[:, 0, 0:128], op=ALU.mult,
                    ).then_inc(sem("msk4"), 1)
                if ch["last"]:
                    vector.wait_ge(sem("rd4"), si + 1)
                    vector.tensor_tensor(
                        out=actT[:, h * T + s * SW: h * T + (s + 1) * SW],
                        in0=ps_ctx[si % 2][:], in1=rdn[si % 2][:], op=ALU.mult,
                    ).then_inc(sem("ce4"), 1)

    es_p4.close()

    if upto == "p4":
        _emit_debug_tail(nc, sem, actT, out_t, dbg, qT_s, kT_s, vT_s, x2_s)
        return nc

    NOS5 = 8
    NG5 = 4
    EP5 = 6
    with nc.Block("p5") as block:
        es_p5 = ExitStack()
        wob = [es_p5.enter_context(nc.sbuf_tensor(f"wob{i}", [128, KC, 512], BF16)) for i in range(2)]
        xcol = [es_p5.enter_context(nc.sbuf_tensor(f"xcol{i}", [128, 512], F32)) for i in range(2)]
        x2col = [es_p5.enter_context(nc.sbuf_tensor(f"x2col{i}", [128, 512], F32)) for i in range(2)]
        sqcol = es_p5.enter_context(nc.sbuf_tensor("sqcol", [128, 512], F32))
        red5 = es_p5.enter_context(nc.sbuf_tensor("red5t", [128, 2], F32))
        ps5 = [es_p5.enter_context(nc.psum_tensor(f"ps5_{i}", [128, 512], F32)) for i in range(8)]

        @block.vector
        def _(vector):
            vector.memset(s1acc[:], 0.0).then_inc(sem("ep5"), 1)
            vector.memset(s2acc[:], 0.0).then_inc(sem("ep5"), 1)

        @block.gpsimd
        def _(gpsimd):
            for os in range(NOS5):
                if os >= 2:
                    gpsimd.wait_ge(sem("mm5"), (os - 1) * NG5)
                for qr in range(4):
                    src = Wo.ap().rearrange("(k p) m -> p k m", p=128)[:, qr * 8:(qr + 1) * 8, os * 512:(os + 1) * 512]
                    gpsimd.dma_start(out=wob[os % 2][:, qr * 8:(qr + 1) * 8, :], in_=src).then_inc(sem("wo5"), 16)

        @block.sync
        def _(sync):
            n = 0
            for os in range(NOS5):
                for g in range(NG5):
                    for tt in range(4):
                        t_g = g * 4 + tt
                        n += 1
                        if n >= 3:
                            sync.wait_ge(sem("ep5"), 2 + (n - 2) * EP5)
                        sync.dma_start(
                            out=xcol[n % 2][:],
                            in_=x_in.ap()[t_g * 128:(t_g + 1) * 128, os * 512:(os + 1) * 512],
                        ).then_inc(sem("xc5"), 16)

        @block.tensor
        def _(tensor):
            for os in range(NOS5):
                tensor.wait_ge(sem("wo5"), 64 * (os + 1))
                for g in range(NG5):
                    gi = os * NG5 + g
                    bank = (gi % 2) * 4
                    if gi >= 2:
                        tensor.wait_ge(sem("ep5"), 2 + ((gi - 2) * 4 + 4) * EP5)
                    for k in range(KC):
                        for tt in range(4):
                            t_g = g * 4 + tt
                            inst = tensor.matmul(
                                ps5[bank + tt][:],
                                actT[:, k * T + t_g * 128: k * T + (t_g + 1) * 128],
                                wob[os % 2][:, k, :],
                                start=(k == 0), stop=(k == KC - 1),
                            )
                            if k == KC - 1 and tt == 3:
                                inst.then_inc(sem("mm5"), 1)

        @block.vector
        def _(vector):
            n = 0
            for os in range(NOS5):
                for g in range(NG5):
                    gi = os * NG5 + g
                    vector.wait_ge(sem("mm5"), gi + 1)
                    bank = (gi % 2) * 4
                    for tt in range(4):
                        t_g = g * 4 + tt
                        n += 1
                        base = 2 + (n - 1) * EP5
                        vector.wait_ge(sem("xc5"), 16 * n)
                        if n >= 3:
                            vector.wait_ge(sem("x2w5"), 16 * (n - 2))
                        ser(vector, "ep5", base + 0, lambda bank=bank, tt=tt, n=n: vector.tensor_tensor(
                            out=x2col[n % 2][:], in0=ps5[bank + tt][:], in1=xcol[n % 2][:], op=ALU.add))
                        ser(vector, "ep5", base + 1, lambda n=n: vector.reduce_sum(
                            out=red5[:, 0:1], in_=x2col[n % 2][:], axis=AX.X))
                        ser(vector, "ep5", base + 3, lambda t_g=t_g: vector.tensor_tensor(
                            out=s1acc[:, t_g:t_g + 1], in0=s1acc[:, t_g:t_g + 1], in1=red5[:, 0:1], op=ALU.add))
                        ser(vector, "ep5", base + 4, lambda t_g=t_g: vector.tensor_tensor(
                            out=s2acc[:, t_g:t_g + 1], in0=s2acc[:, t_g:t_g + 1], in1=red5[:, 1:2], op=ALU.add))

        @block.scalar
        def _(scalar):
            n = 0
            for os in range(NOS5):
                for g in range(NG5):
                    for tt in range(4):
                        t_g = g * 4 + tt
                        n += 1
                        base = 2 + (n - 1) * EP5
                        ser(scalar, "ep5", base + 2, lambda n=n: scalar.activation(
                            out=sqcol[:], in_=x2col[n % 2][:], func=AF.Square,
                            accum_out=red5[:, 1:2]))
                        scalar.wait_ge(sem("ep5"), base + 5)
                        scalar.dma_start(
                            out=x2_s.ap()[t_g * 128:(t_g + 1) * 128, os * 512:(os + 1) * 512],
                            in_=x2col[n % 2][:],
                        ).then_inc(sem("x2w5"), 16)
                        scalar.nop().then_inc(sem("ep5"), 1)

        es_p5.close()

    if upto == "p5":
        _emit_debug_tail(nc, sem, actT, out_t, dbg, qT_s, kT_s, vT_s, x2_s)
        return nc

    es_c = ExitStack()
    AeffT = es_c.enter_context(nc.sbuf_tensor("AeffT", [128, KC * 16], BF16))
    BeffT = es_c.enter_context(nc.sbuf_tensor("BeffT", [16, KC * 128], BF16))
    tT_sb = es_c.enter_context(nc.sbuf_tensor("tT_sb", [16, T], BF16))

    with nc.Block("p6") as block:
        es_p6 = ExitStack()
        x2r = [es_p6.enter_context(nc.sbuf_tensor("x2r0", [128, D], F32))] * 2
        h2b = [es_p6.enter_context(nc.sbuf_tensor("h2b0", [128, D], BF16))] * 2
        Ast = es_p6.enter_context(nc.sbuf_tensor("Ast", [16, 3 * 1024], F32))
        aeff_c = es_p6.enter_context(nc.sbuf_tensor("aeff_c", [16, 1024], F32))
        aeffb = es_p6.enter_context(nc.sbuf_tensor("aeffb", [16, D], BF16))
        B_sb = es_p6.enter_context(nc.sbuf_tensor("B_sb", [128, 3, KC, 16], F32))
        beffall = es_p6.enter_context(nc.sbuf_tensor("beffall", [128, KC * 16], BF16))
        mv_sb = es_p6.enter_context(nc.sbuf_tensor("mv_sb", [128, 3], F32))
        pt6 = [es_p6.enter_context(nc.psum_tensor(f"pt6_{i}", [128, 512], BF16)) for i in range(2)]
        ps6m = es_p6.enter_context(nc.psum_tensor("ps6m", [128, 512], F32))
        pt6b = es_p6.enter_context(nc.psum_tensor("pt6b", [128, 512], BF16))

        @block.vector
        def _(vector):
            ser(vector, "z6", 0, lambda: vector.tensor_scalar_mul(out=s1acc[:], in0=s1acc[:], scalar1=1.0 / D))
            ser(vector, "z6", 1, lambda: vector.tensor_scalar_mul(out=s2acc[:], in0=s2acc[:], scalar1=1.0 / D))
            ser(vector, "z6", 2, lambda: vector.tensor_tensor(out=rstd2[:], in0=s1acc[:], in1=s1acc[:], op=ALU.mult))
            ser(vector, "z6", 3, lambda: vector.tensor_tensor(out=rstd2[:], in0=rstd2[:], in1=s2acc[:], op=ALU.subtract))
            ser(vector, "z6", 4, lambda: vector.tensor_scalar(
                out=rstd2[:], in0=rstd2[:], scalar1=-1.0, scalar2=EPS, op0=ALU.mult, op1=ALU.add))
            ser(vector, "z6", 6, lambda: vector.reciprocal(out=rstd2[:], in_=negmr2[:]))
            ser(vector, "z6", 7, lambda: vector.scalar_tensor_tensor(
                out=negmr2[:], in0=s1acc[:], scalar=-1.0, in1=rstd2[:], op0=ALU.mult, op1=ALU.mult))

        @block.scalar
        def _(scalar):
            ser(scalar, "z6", 5, lambda: scalar.activation(
                out=negmr2[:], in_=rstd2[:], func=AF.Sqrt))

        @block.sync
        def _(sync):
            sync.wait_ge(sem("x2w5"), 16 * 128)
            for t in range(NT):
                if t >= 1:
                    sync.wait_ge(sem("ap6"), t)
                sync.dma_start(out=x2r[t % 2][:], in_=x2_s.ap()[t * 128:(t + 1) * 128, :]).then_inc(sem("xl6"), 16)

        @block.scalar
        def _(scalar):
            scalar.wait_ge(sem("z6"), 8)
            for t in range(NT):
                i = t % 2
                scalar.wait_ge(sem("xl6"), 16 * (t + 1))
                if t >= 1:
                    scalar.wait_ge(sem("tr6"), 8 * t)
                scalar.activation(
                    out=h2b[i][:], in_=x2r[i][:], func=AF.Identity,
                    scale=rstd2[:, t:t + 1], bias=negmr2[:, t:t + 1],
                ).then_inc(sem("ap6"), 1)
                for g in range(8):
                    gi = t * 8 + g
                    scalar.wait_ge(sem("tr6"), gi + 1)
                    dst = actT.ap().rearrange("p (c q) -> p c q", q=T)[:, 4 * g:4 * g + 4, t * 128:(t + 1) * 128]
                    scalar.copy(out=dst, in_=pt6[g % 2].ap().rearrange("p (c q) -> p c q", q=128)).then_inc(sem("ev6"), 1)

        @block.tensor
        def _(tensor):
            for t in range(NT):
                i = t % 2
                tensor.wait_ge(sem("ap6"), t + 1)
                for g in range(8):
                    gi = t * 8 + g
                    if gi >= 2:
                        tensor.wait_ge(sem("ev6"), gi - 1)
                    for j in range(4):
                        c = 4 * g + j
                        inst = tensor.transpose(
                            pt6[g % 2][:, j * 128:(j + 1) * 128],
                            h2b[i][:, c * 128:(c + 1) * 128],
                            idt[:],
                        )
                        if j == 3:
                            inst.then_inc(sem("tr6"), 1)

        lops = []
        cast_slot = {}

        def f_bcast(e):
            e.wait_ge(sem("z0"), 25)
            return e.matmul(ps6m[:, 0:3], onesf[:], mrow[:], start=True, stop=True)
        lops.append(("tensor", f_bcast))
        lops.append(("scalar", lambda e: e.copy(out=mv_sb[:], in_=ps6m[:, 0:3])))
        for cc in range(4):
            def f_a0(e, cc=cc):
                e.wait_ge(sem("g6a"), 48 * (cc + 1))
                return e.tensor_scalar(out=aeff_c[0:16, :], in0=Ast[0:16, 0:1024],
                                       scalar1=mv_sb[0:16, 0:1], scalar2=None, op0=ALU.mult)
            lops.append(("vector", f_a0))
            lops.append(("vector", lambda e: e.scalar_tensor_tensor(
                out=aeff_c[0:16, :], in0=Ast[0:16, 1024:2048], scalar=mv_sb[0:16, 1:2],
                in1=aeff_c[0:16, :], op0=ALU.mult, op1=ALU.add)))
            lops.append(("vector", lambda e: e.scalar_tensor_tensor(
                out=aeff_c[0:16, :], in0=Ast[0:16, 2048:3072], scalar=mv_sb[0:16, 2:3],
                in1=aeff_c[0:16, :], op0=ALU.mult, op1=ALU.add)))
            lops.append(("vector", lambda e, cc=cc: e.tensor_copy(
                out=aeffb[0:16, cc * 1024:(cc + 1) * 1024], in_=aeff_c[0:16, :])))
            cast_slot[cc] = len(lops) - 1
        for c in range(KC):
            lops.append(("tensor", lambda e, c=c: e.transpose(
                pt6b[:, c * 16:(c + 1) * 16], aeffb[0:16, c * 128:(c + 1) * 128], idt[0:16, 0:16])))
        lops.append(("scalar", lambda e: e.copy(out=AeffT[:], in_=pt6b[:, 0:512])))
        def bsl(a):
            return B_sb.ap().rearrange("p a c r -> p a (c r)")[:, a, :]

        def f_b0(e):
            e.wait_ge(sem("g6"), 48)
            return e.tensor_scalar(out=beffall[:], in0=bsl(0),
                                   scalar1=mv_sb[:, 0:1], scalar2=None, op0=ALU.mult)
        lops.append(("vector", f_b0))
        lops.append(("vector", lambda e: e.scalar_tensor_tensor(
            out=beffall[:], in0=bsl(1), scalar=mv_sb[:, 1:2], in1=beffall[:],
            op0=ALU.mult, op1=ALU.add)))
        lops.append(("vector", lambda e: e.scalar_tensor_tensor(
            out=beffall[:], in0=bsl(2), scalar=mv_sb[:, 2:3], in1=beffall[:],
            op0=ALU.mult, op1=ALU.add)))
        for c in range(KC):
            lops.append(("tensor", lambda e, c=c: e.transpose(
                pt6b[0:16, (c % 4) * 128:((c % 4) + 1) * 128], beffall[:, c * 16:(c + 1) * 16], idt[:])))
            if c % 4 == 3:
                lops.append(("scalar", lambda e, c=c: e.copy(
                    out=BeffT[0:16, (c - 3) * 128:(c + 1) * 128], in_=pt6b[0:16, 0:512])))
        run_oplist(block, "z6b", lops)

        @block.gpsimd
        def _(gpsimd):
            for a in range(3):
                gpsimd.dma_start(
                    out=B_sb[:, a, :, :],
                    in_=lB.ap()[a].rearrange("(c p) r -> p c r", p=128),
                ).then_inc(sem("g6"), 16)
            for cc in range(4):
                if cc >= 1:
                    gpsimd.wait_ge(sem("z6b"), cast_slot[cc - 1] + 1)
                for a in range(3):
                    gpsimd.dma_start(
                        out=Ast[0:16, a * 1024:(a + 1) * 1024],
                        in_=lA.ap()[a][:, cc * 1024:(cc + 1) * 1024],
                    ).then_inc(sem("g6a"), 16)

        es_p6.close()

    if upto == "p6":
        _emit_debug_tail(nc, sem, actT, out_t, dbg, qT_s, kT_s, vT_s, x2_s)
        return nc

    with nc.Block("p7") as block:
        es_p7 = ExitStack()
        wfth = [es_p7.enter_context(nc.sbuf_tensor(f"wfth{i}", [128, 16, 512], BF16)) for i in range(3)]
        bias_sb = es_p7.enter_context(nc.sbuf_tensor("bias_sb", [1, D], BF16))
        onesfb = es_p7.enter_context(nc.sbuf_tensor("onesfb", [1, 128], BF16))
        x2c7 = [es_p7.enter_context(nc.sbuf_tensor(f"x2c7_{i}", [128, 512], F32)) for i in range(2)]
        outc = [es_p7.enter_context(nc.sbuf_tensor(f"outc{i}", [128, 512], F32)) for i in range(2)]
        ps7 = [es_p7.enter_context(nc.psum_tensor(f"ps7_{i}", [128, 512], F32)) for i in range(8)]

        @block.scalar
        def _(scalar):
            for ts in range(4):
                scalar.wait_ge(sem("mm7"), ts + 1)
                scalar.copy(out=tT_sb[0:16, ts * 512:(ts + 1) * 512], in_=ps7[0][0:16, :]).then_inc(sem("tT7"), 1)

        @block.gpsimd
        def _(gpsimd):
            gpsimd.dma_start(out=bias_sb[:], in_=bias_t.ap()).then_inc(sem("l7"), 16)

        @block.scalar
        def _(scalar):
            scalar.dma_start(out=onesfb[:], in_=onesq_c.ap()[0:1, :]).then_inc(sem("l7"), 16)
            coords = []
            for os in range(8):
                for g in range(4):
                    for tt in range(4):
                        coords.append((g * 4 + tt, os))
            for n in (1, 2):
                t_g, os = coords[n - 1]
                scalar.dma_start(
                    out=x2c7[n % 2][:],
                    in_=x2_s.ap()[t_g * 128:(t_g + 1) * 128, os * 512:(os + 1) * 512],
                ).then_inc(sem("x2c7"), 16)
            for n_out in range(1, 129):
                if n_out + 2 <= 128:
                    n = n_out + 2
                    t_g, os = coords[n - 1]
                    scalar.wait_ge(sem("ev7"), n - 2)
                    scalar.dma_start(
                        out=x2c7[n % 2][:],
                        in_=x2_s.ap()[t_g * 128:(t_g + 1) * 128, os * 512:(os + 1) * 512],
                    ).then_inc(sem("x2c7"), 16)
                t_g, os = coords[n_out - 1]
                scalar.wait_ge(sem("ev7"), n_out)
                scalar.dma_start(
                    out=out_t.ap()[t_g * 128:(t_g + 1) * 128, os * 512:(os + 1) * 512],
                    in_=outc[n_out % 2][:],
                ).then_inc(sem("out7"), 16)

        @block.tensor
        def _(tensor):
            for ts in range(4):
                if ts >= 1:
                    tensor.wait_ge(sem("tT7"), ts)
                for k in range(KC):
                    inst = tensor.matmul(
                        ps7[0][0:16, :], AeffT[:, k * 16:(k + 1) * 16],
                        actT[:, k * T + ts * 512: k * T + (ts + 1) * 512],
                        start=(k == 0), stop=(k == KC - 1),
                    )
                    if k == KC - 1:
                        inst.then_inc(sem("mm7"), 1)
            for os in range(8):
                for g in range(4):
                    gi = os * 4 + g
                    bank = (gi % 2) * 4
                    if gi == 0:
                        tensor.wait_ge(sem("tT7"), 4)
                        tensor.wait_ge(sem("l7"), 32)
                    if gi >= 2:
                        tensor.wait_ge(sem("ev7"), (gi - 2) * 4 + 4)
                    for k in range(KC):
                        hh = os * 2 + k // 16
                        tensor.wait_ge(sem("wft7"), 16 * (os * KC + k + 1))
                        for tt in range(4):
                            t_g = g * 4 + tt
                            inst = tensor.matmul(
                                ps7[bank + tt][:],
                                actT[:, k * T + t_g * 128: k * T + (t_g + 1) * 128],
                                wfth[hh % 3][:, k % 16, :],
                                start=(k == 0), stop=False,
                            )
                            if g == 3 and tt == 3 and k % 16 == 15:
                                inst.then_inc(sem("osd7"), 1)
                    for tt in range(4):
                        t_g = g * 4 + tt
                        tensor.matmul(
                            ps7[bank + tt][:], onesfb[:], bias_sb[0:1, os * 512:(os + 1) * 512],
                            start=False, stop=False,
                        )
                        inst = tensor.matmul(
                            ps7[bank + tt][:], tT_sb[0:16, t_g * 128:(t_g + 1) * 128],
                            BeffT[0:16, os * 512:(os + 1) * 512],
                            start=False, stop=True,
                        )
                        if tt == 3:
                            inst.then_inc(sem("mm7"), 1)


        @block.sync
        def _(sync):
            sync.wait_ge(sem("wfw2"), 16 * 32)
            sync.wait_ge(sem("l7"), 32)
            for os in range(8):
                for half in range(2):
                    hh = os * 2 + half
                    if hh >= 3:
                        sync.wait_ge(sem("osd7"), hh - 2)
                    for kl in range(16):
                        k = half * 16 + kl
                        sync.dma_start(
                            out=wfth[hh % 3][:, kl, :],
                            in_=WfB.ap()[os * 512:(os + 1) * 512, k * 128:(k + 1) * 128],
                            transpose=True,
                        ).then_inc(sem("wft7"), 16)

        @block.vector
        def _(vector):
            n = 0
            for os in range(8):
                for g in range(4):
                    gi = os * 4 + g
                    bank = (gi % 2) * 4
                    vector.wait_ge(sem("mm7"), 4 + gi + 1)
                    for tt in range(4):
                        n += 1
                        vector.wait_ge(sem("x2c7"), 16 * n)
                        if n >= 3:
                            vector.wait_ge(sem("out7"), 16 * (n - 2))
                        vector.tensor_tensor(
                            out=outc[n % 2][:], in0=ps7[bank + tt][:], in1=x2c7[n % 2][:], op=ALU.add,
                        ).then_inc(sem("ev7"), 1)

        @block.gpsimd
        def _(gpsimd):
            gpsimd.wait_ge(sem("out7"), 16 * 128)

        es_p7.close()

    return nc


def _emit_debug_tail(nc, sem, actT, out_t, dbg, qT_s, kT_s, vT_s, x2_s):
    with nc.Block("dbgtail") as block:

        @block.gpsimd
        def _(gpsimd):
            dst = out_t.ap().rearrange("(a p) m -> p a m", p=128)
            gpsimd.dma_start(out=dst, in_=actT.ap().rearrange("p (a m) -> p a m", m=D)).then_inc(sem("dbg1"), 16)
            if dbg:
                for name, scr in [("qT", qT_s), ("kT", kT_s), ("vT", vT_s)]:
                    if name in dbg:
                        s = scr.ap()
                        d = dbg[name].ap()
                        gpsimd.dma_start(out=d, in_=s).then_inc(sem("dbg1"), 16)
                if "x2" in dbg:
                    gpsimd.dma_start(out=dbg["x2"].ap(), in_=x2_s.ap()).then_inc(sem("dbg1"), 16)
            n = 1 + (len(dbg) if dbg else 0)
            gpsimd.wait_ge(sem("dbg1"), 16 * n)


_NC_CACHE = {}


def _get_nc():
    if "nc" not in _NC_CACHE:
        _NC_CACHE["nc"] = build_nc()
    return _NC_CACHE["nc"]


def make_in_maps(inputs):
    x = np.ascontiguousarray(np.asarray(inputs["L0_attn_x"], dtype=np.float32))
    Wq = np.ascontiguousarray(np.asarray(inputs["L0_attn_Wq"], dtype=np.float32))
    Wk = np.ascontiguousarray(np.asarray(inputs["L0_attn_Wk"], dtype=np.float32))
    Wv = np.ascontiguousarray(np.asarray(inputs["L0_attn_Wv"], dtype=np.float32))
    Wo = np.ascontiguousarray(np.asarray(inputs["L0_attn_Wo"], dtype=np.float32))
    Wf = np.ascontiguousarray(np.asarray(inputs["L0_ffn_weight"], dtype=np.float32))
    fb = np.ascontiguousarray(np.asarray(inputs["L0_ffn_bias"], dtype=np.float32)).reshape(1, D)
    lAv = np.ascontiguousarray(np.asarray(inputs["L0_ffn_lora_A"], dtype=np.float32))
    lBv = np.ascontiguousarray(np.asarray(inputs["L0_ffn_lora_B"], dtype=np.float32))
    lsi = np.ascontiguousarray(np.asarray(inputs["L0_ffn_lora_start_indices"], dtype=np.int32)).reshape(1, 4)
    in_maps = []
    for b in range(B):
        bsel = np.zeros((1, 8), dtype=np.float32)
        bsel[0, b] = 1.0
        in_maps.append({
            "x": np.ascontiguousarray(x[b]), "Wq": Wq, "Wk": Wk, "Wv": Wv,
            "Wo": Wo, "Wf": Wf, "fbias": fb, "lA": lAv, "lB": lBv,
            "lsi": lsi, "bsel": bsel,
        })
    return in_maps


def kernel(**inputs):
    nc = _get_nc()
    in_maps = make_in_maps(inputs)
    res = run_bass_kernel_spmd(nc, in_maps, core_ids=list(range(B)))
    out = np.stack([res.results[b]["out"] for b in range(B)], axis=0)
    return out.astype(np.float32)
